# version 19
# baseline (speedup 1.0000x reference)
import numpy as np
import ml_dtypes
from contextlib import ExitStack

import concourse.bass as bass
import concourse.mybir as mybir
from concourse.bass_utils import run_bass_kernel_spmd

F32 = mybir.dt.float32
BF16 = mybir.dt.bfloat16
ALU = mybir.AluOpType
AF = mybir.ActivationFunctionType
AX = mybir.AxisListType

MAGIC = 12582912.0
QMAX = 127.0
D, I3, H3 = 1024, 2816, 3072
DC, IC, PC = 8, 22, 24
NT, T, TOK = 8, 512, 4096
EPS = 1e-5


class Sem:
    def __init__(self, ctx, nc, name):
        self.h = ctx.enter_context(nc.semaphore(name))
        self.v = 0


class DrainingEngine:
    OPS = frozenset([
        "tensor_tensor", "tensor_scalar", "tensor_copy", "tensor_reduce",
        "reduce_max", "reduce_sum", "scalar_tensor_tensor", "tensor_tensor_scan",
        "reciprocal", "memset", "activation", "copy", "add", "mul", "select",
        "tensor_add", "tensor_sub", "tensor_mul", "tensor_max", "transpose",
        "tensor_single_scalar", "tensor_scalar_mul", "tensor_scalar_add",
    ])

    def __init__(self, eng):
        object.__setattr__(self, "_e", eng)

    def __getattr__(self, name):
        attr = getattr(self._e, name)
        if name in DrainingEngine.OPS:
            def wrapped(*a, **kw):
                self._e.drain()
                return attr(*a, **kw)
            return wrapped
        return attr


def build():
    nc = bass.Bass("TRN2", target_bir_lowering=False)
    vec = DrainingEngine(nc.vector)
    scl = DrainingEngine(nc.scalar)
    pool = DrainingEngine(nc.gpsimd)

    x_fm = nc.dram_tensor("x_fm", [D, TOK], F32, kind="ExternalInput").ap()
    w_gate = nc.dram_tensor("w_gate", [I3, D], F32, kind="ExternalInput").ap()
    w_up = nc.dram_tensor("w_up", [I3, D], F32, kind="ExternalInput").ap()
    w_down = nc.dram_tensor("w_down", [D, I3], F32, kind="ExternalInput").ap()
    W_attn = nc.dram_tensor("W_attn", [D, H3], F32, kind="ExternalInput").ap()
    b_attn = nc.dram_tensor("b_attn", [PC, 128], F32, kind="ExternalInput").ap()
    clips = nc.dram_tensor("clips", [1, 2], F32, kind="ExternalInput").ap()
    flags = nc.dram_tensor("flags", [1, 2], F32, kind="ExternalInput").ap()
    identin = nc.dram_tensor("identin", [128, 128], F32, kind="ExternalInput").ap()
    out_fm = nc.dram_tensor("out_fm", [D, TOK], F32, kind="ExternalOutput").ap()

    gq = nc.dram_tensor("gq", [I3, D], BF16).ap()
    uq = nc.dram_tensor("uq", [I3, D], BF16).ap()
    dq = nc.dram_tensor("dq", [D, I3], BF16).ap()
    Wb = nc.dram_tensor("Wbs", [D, H3], BF16).ap()
    h_scr = nc.dram_tensor("h_scr", [NT * 128, IC * T], BF16).ap()
    x1_scr = nc.dram_tensor("x1_scr", [NT * 128, TOK], BF16).ap()
    x2_scr = nc.dram_tensor("x2_scr", [NT * 128, TOK], BF16).ap()
    xn2_scr = nc.dram_tensor("xn2_scr", [NT * 128, TOK], BF16).ap()
    hl_scr = nc.dram_tensor("hl_scr", [NT * 128, TOK], BF16).ap()
    pp_scr = nc.dram_tensor("pp_scr", [NT * 128, TOK], BF16).ap()
    og_scr = nc.dram_tensor("og_scr", [NT * 128, TOK], BF16).ap()
    cmax_i = [nc.dram_tensor(f"cmax_i{i}", [1, 8], F32).ap() for i in range(4)]
    cmax_o = [nc.dram_tensor(f"cmax_o{i}", [1, 8], F32).ap() for i in range(4)]
    carry_i = nc.dram_tensor("carry_i", [8, 128], F32).ap()
    carry_o = nc.dram_tensor("carry_o", [8, 128], F32).ap()

    x3d = x_fm.rearrange("(c p) n -> p c n", p=128)
    o3d = out_fm.rearrange("(c p) n -> p c n", p=128)

    ctx = ExitStack()
    with ctx:
        s0 = Sem(ctx, nc, "s0")
        wldp = [Sem(ctx, nc, f"wld{i}") for i in range(2)]
        wst = Sem(ctx, nc, "wst")
        wgu = Sem(ctx, nc, "wgu")
        qd = Sem(ctx, nc, "qdm")
        wb = Sem(ctx, nc, "wbm")
        sx = Sem(ctx, nc, "sx")
        shl = Sem(ctx, nc, "shl")
        shstp = [Sem(ctx, nc, f"shst{i}") for i in range(2)]
        sx1p = [Sem(ctx, nc, f"sx1p{i}") for i in range(2)]
        sxn2 = Sem(ctx, nc, "sxn2")
        sout = Sem(ctx, nc, "sout")
        sxl = [Sem(ctx, nc, f"sxl{i}") for i in range(2)]
        sast = Sem(ctx, nc, "sast")
        spb = [Sem(ctx, nc, f"spb{i}") for i in range(2)]
        sx2p = [Sem(ctx, nc, f"sx2p{i}") for i in range(2)]
        qg = Sem(ctx, nc, "qg")
        sxg = [Sem(ctx, nc, f"sxg{i}") for i in range(2)]
        cc = Sem(ctx, nc, "cc")
        pe = Sem(ctx, nc, "pe")
        dve = Sem(ctx, nc, "dve")
        act = Sem(ctx, nc, "act")
        gp = Sem(ctx, nc, "gp")
        sy = Sem(ctx, nc, "sy")
        ALL_DMA_SEMS = [s0, wldp[0], wldp[1], wst, wgu, qd, wb, sx, shl, shstp[0], shstp[1],
                        sx1p[0], sx1p[1], sxn2, sout, sxl[0], sxl[1], sast,
                        spb[0], spb[1], sx2p[0], sx2p[1], qg, sxg[0], sxg[1]]

        def dma(q, out, in_, sem, **kw):
            q.dma_start(out=out, in_=in_, **kw).then_inc(sem.h, 16)
            sem.v += 16
            return sem.v

        def tick(instr, sem):
            sem.v += 1
            instr.then_inc(sem.h, 1)
            return sem.v

        ENGS = [nc.tensor, nc.vector, nc.scalar, nc.gpsimd, nc.sync]

        def fence_snapshot():
            return [(s.h, s.v) for s in [pe, dve, act, gp] + ALL_DMA_SEMS]

        def queue_fence(snap):
            for eng in (nc.sync, nc.scalar):
                for h, v in snap:
                    if v > 0:
                        eng.wait_ge(h, v)

        def barrier():
            snap = [(s.h, s.v) for s in [pe, dve, act, gp, cc] + ALL_DMA_SEMS]
            for eng in ENGS:
                for h, v in snap:
                    if v > 0:
                        eng.wait_ge(h, v)

        ones128 = ctx.enter_context(nc.sbuf_tensor("ones128", [128, 128], BF16))
        onesrow = ctx.enter_context(nc.sbuf_tensor("onesrow", [1, 128], F32))
        identf = ctx.enter_context(nc.sbuf_tensor("identf", [128, 128], F32))
        zerosT = ctx.enter_context(nc.sbuf_tensor("zerosT", [128, T], BF16))
        bias_t = ctx.enter_context(nc.sbuf_tensor("bias_t", [128, PC], F32))
        negb_t = ctx.enter_context(nc.sbuf_tensor("negb_t", [128, PC], F32))
        sclips = ctx.enter_context(nc.sbuf_tensor("sclips", [1, 2], F32))
        sg_t = ctx.enter_context(nc.sbuf_tensor("sg_t", [128, IC], F32))
        su_t = ctx.enter_context(nc.sbuf_tensor("su_t", [128, IC], F32))
        sd_t = ctx.enter_context(nc.sbuf_tensor("sd_t", [128, DC], F32))
        sgc = [ctx.enter_context(nc.sbuf_tensor(f"sgc{i}", [128, IC], F32)) for i in range(2)]
        suc = [ctx.enter_context(nc.sbuf_tensor(f"suc{i}", [128, IC], F32)) for i in range(2)]
        sdc = [ctx.enter_context(nc.sbuf_tensor(f"sdc{i}", [128, DC], F32)) for i in range(2)]
        sv = [ctx.enter_context(nc.sbuf_tensor(f"sv{i}", [128, 2], F32)) for i in range(4)]
        flv = ctx.enter_context(nc.sbuf_tensor("flv", [128, 2], F32))
        state_h = ctx.enter_context(nc.sbuf_tensor("state_h", [128, DC], F32))
        state_p = ctx.enter_context(nc.sbuf_tensor("state_p", [128, DC], F32))
        h_in_t = ctx.enter_context(nc.sbuf_tensor("h_in_t", [128, DC], F32))
        amax8 = [ctx.enter_context(nc.sbuf_tensor(f"amax8_{i}", [128, NT * IC], F32)) for i in range(4)]
        cmax_sb = ctx.enter_context(nc.sbuf_tensor("cmax_sb", [1, 8], F32))
        gmax_sb = ctx.enter_context(nc.sbuf_tensor("gmax_sb", [1, 8], F32))
        srow = ctx.enter_context(nc.sbuf_tensor("srow", [1, 2], F32))
        carr_sb = ctx.enter_context(nc.sbuf_tensor("carr_sb", [8, 128], F32))
        lm_g = ctx.enter_context(nc.sbuf_tensor("lm_g", [128, 1], F32))
        biasC = ctx.enter_context(nc.sbuf_tensor("biasC", [128, 1], F32))
        negC = ctx.enter_context(nc.sbuf_tensor("negC", [128, 1], F32))
        XN = {}

        def xs(t, c):
            off = t * (DC * T) + c * T
            return XN["t"][:, off:off + T]

        def xnsl(t):
            return XN["t"][:, t * (DC * T):(t + 1) * (DC * T)]

        v_ident = dma(nc.sync, identf[:], identin, s0)
        v_clip = dma(nc.sync, sclips[:], clips, s0)
        vec.memset(ones128[:], 1.0)
        vec.memset(onesrow[:], 1.0)
        vec.memset(zerosT[:], 0.0)
        vec.memset(state_h[:], 0.0)
        vec.memset(state_p[:], 1.0)
        vec.memset(cmax_sb[:], 0.0)
        v_memset = tick(vec.memset(srow[:], 0.0), dve)

        with nc.sbuf_tensor("btmp", [PC, 128], F32) as btmp, \
             nc.sbuf_tensor("frow", [1, 2], F32) as frow, \
             nc.psum_tensor("pb", [128, PC], F32) as pb, \
             nc.psum_tensor("pf", [128, 2], F32) as pf:
            v_b = dma(nc.sync, btmp[:], b_attn, s0)
            v_f = dma(nc.sync, frow[:], flags, s0)
            scl.wait_ge(s0.h, s0.v)
            v_sig = tick(scl.activation(out=sclips[:], in_=sclips[:], func=AF.Sigmoid), act)
            nc.tensor.wait_ge(s0.h, s0.v)
            nc.tensor.wait_ge(dve.h, v_memset)
            nc.tensor.transpose(pb[:], btmp[:], identf[0:PC, 0:PC])
            v_pe0 = tick(nc.tensor.matmul(pf[:], onesrow[:], frow[:], start=True, stop=True), pe)
            vec.wait_ge(pe.h, v_pe0)
            vec.tensor_copy(out=bias_t[:], in_=pb[:])
            vec.tensor_scalar(out=negb_t[:], in0=pb[:], scalar1=-1.0, scalar2=None, op0=ALU.mult)
            v_flv = tick(vec.tensor_copy(out=flv[:], in_=pf[:]), dve)

        def ln_front(xb_tile, sq_tile, pstat):
            v_sq = tick(vec.tensor_tensor(out=sq_tile[:], in0=xb_tile[:],
                                          in1=xb_tile[:], op=ALU.mult), dve)
            nc.tensor.wait_ge(dve.h, v_sq)
            for c in range(DC):
                nc.tensor.matmul(pstat[:, 0:T], ones128[:], xb_tile[:, c * T:(c + 1) * T],
                                 start=(c == 0), stop=(c == DC - 1))
            for c in range(DC):
                mms = nc.tensor.matmul(pstat[:, T:2 * T], ones128[:], sq_tile[:, c * T:(c + 1) * T],
                                       start=(c == 0), stop=(c == DC - 1))
            return tick(mms, pe)

        def ln_back(xb_tile, pstat, mrows, xn_out, v_st):
            m_t, var_t, rstd_t, mb, rstdb = (mrows["m"], mrows["var"], mrows["rstd"],
                                             mrows["mb"], mrows["rstdb"])
            vec.wait_ge(pe.h, v_st)
            vec.tensor_scalar(out=m_t[:], in0=pstat[:, 0:T], scalar1=1.0 / D,
                              scalar2=None, op0=ALU.mult)
            vec.tensor_scalar(out=var_t[:], in0=pstat[:, T:2 * T], scalar1=1.0 / D,
                              scalar2=EPS, op0=ALU.mult, op1=ALU.add)
            vec.tensor_tensor(out=rstd_t[:], in0=m_t[:], in1=m_t[:], op=ALU.mult)
            v_var = tick(vec.tensor_tensor(out=var_t[:], in0=var_t[:], in1=rstd_t[:],
                                           op=ALU.subtract), dve)
            scl.wait_ge(dve.h, v_var)
            scl.activation(out=rstd_t[:], in_=var_t[:], func=AF.Ln)
            v_rs = tick(scl.activation(out=rstd_t[:], in_=rstd_t[:], func=AF.Exp,
                                       scale=-0.5), act)
            vec.wait_ge(act.h, v_rs)
            vec.tensor_copy(out=mb[:], in_=m_t[:])
            vec.tensor_copy(out=rstdb[:], in_=rstd_t[:])
            xo3 = xn_out.rearrange("p (c n) -> p c n", n=T)
            mbb = mb[:].unsqueeze(1).broadcast_to([128, DC, T])
            rbb = rstdb[:].unsqueeze(1).broadcast_to([128, DC, T])
            vec.tensor_tensor(out=xo3, in0=xb_tile[:].rearrange("p (c n) -> p c n", n=T),
                              in1=mbb, op=ALU.subtract)
            return tick(vec.tensor_tensor(out=xo3, in0=xo3, in1=rbb, op=ALU.mult), dve)

        def global_max_barrier(idx, amax_tile, n_cols):
            with nc.psum_tensor(f"pt{idx}", [1, 128], F32) as pt, \
                 nc.psum_tensor(f"pv{idx}", [128, 2], F32) as pv:
                lm = lm_g
                v0 = tick(vec.tensor_reduce(out=lm[:], in_=amax_tile[:, 0:n_cols],
                                                  axis=AX.X, op=ALU.max), dve)
                nc.tensor.wait_ge(dve.h, v0)
                nc.tensor.wait_ge(act.h, act.v)
                v1 = tick(nc.tensor.transpose(pt[:], lm[:], identf[:]), pe)
                vec.wait_ge(pe.h, v1)
                v2 = tick(vec.tensor_reduce(out=cmax_sb[0:1, 0:1], in_=pt[:],
                                                  axis=AX.X, op=ALU.max), dve)
                nc.gpsimd.wait_ge(dve.h, v2)
                v3 = dma(nc.gpsimd, cmax_i[idx], cmax_sb[:], qg)
                nc.gpsimd.wait_ge(qg.h, v3)
                cinst = nc.gpsimd.collective_compute(
                    "AllReduce", ALU.max, replica_groups=[list(range(8))],
                    ins=[cmax_i[idx]], outs=[cmax_o[idx]])
                v4 = tick(cinst, cc)
                nc.gpsimd.wait_ge(cc.h, v4)
                v5 = dma(nc.gpsimd, gmax_sb[:], cmax_o[idx], qg)
                vec.wait_ge(qg.h, v5)
                clip_col = 0 if idx in (0, 2) else 1
                vec.tensor_tensor(out=srow[0:1, 0:1], in0=gmax_sb[0:1, 0:1],
                                        in1=sclips[0:1, clip_col:clip_col + 1], op=ALU.mult)
                vec.tensor_scalar(out=srow[0:1, 0:1], in0=srow[0:1, 0:1],
                                        scalar1=1.0 / QMAX, scalar2=None, op0=ALU.mult)
                v6 = tick(vec.reciprocal(out=srow[0:1, 1:2], in_=srow[0:1, 0:1]), dve)
                nc.tensor.wait_ge(dve.h, v6)
                v7 = tick(nc.tensor.matmul(pv[:], onesrow[:], srow[:], start=True, stop=True), pe)
                vec.wait_ge(pe.h, v7)
                v8 = tick(vec.tensor_copy(out=sv[idx][:], in_=pv[:]), dve)
            return v8

        def quant_resident(svec, n_blocks, tscr):
            for blk in range(n_blocks):
                sl = xnsl(blk)
                vec.tensor_scalar(out=tscr[:, 0:TOK], in0=sl, scalar1=svec[:, 1:2],
                                        scalar2=MAGIC, op0=ALU.mult, op1=ALU.add)
                vec.tensor_scalar(out=tscr[:, 0:TOK], in0=tscr[:, 0:TOK],
                                        scalar1=MAGIC - 128.0, scalar2=MAGIC + 127.0,
                                        op0=ALU.max, op1=ALU.min)
                tick(vec.tensor_scalar(out=sl, in0=tscr[:, 0:TOK], scalar1=MAGIC,
                                             scalar2=None, op0=ALU.subtract), dve)

        seg1 = ExitStack()
        XN["t"] = seg1.enter_context(nc.sbuf_tensor("xn_seg1", [128, DC * TOK], BF16))
        with ExitStack() as pctx:
            sb = lambda n, s, d: pctx.enter_context(nc.sbuf_tensor(n, s, d))
            WBW = 11264
            wraw = [sb(f"wraw{j}", [128, WBW], BF16) for j in range(2)]
            qout = sb("qout", [128, WBW], BF16)
            tq = [sb("tq0", [128, 2816], F32)] * 2
            rmax = sb("rmax", [128, 1], F32)
            rsvt2 = [sb(f"rsvt{j}", [128, 12], F32) for j in range(2)]
            x0b = [sb(f"x0b{j}", [128, TOK], BF16) for j in range(2)]
            sqb = [sb(f"sqb{j}", [128, TOK], BF16) for j in range(2)]
            mrows2 = []
            for j in range(2):
                mrows2.append(dict(
                    m=sb(f"m_t{j}", [128, T], F32), var=sb(f"var_t{j}", [128, T], F32),
                    rstd=sb(f"rstd_t{j}", [128, T], F32), mb=sb(f"mb{j}", [128, T], BF16),
                    rstdb=sb(f"rstdb{j}", [128, T], BF16), t1=sb(f"t1b{j}", [128, T], BF16)))
            pstat1 = [pctx.enter_context(nc.psum_tensor(f"pstat1_{j}", [128, 2 * T], F32))
                      for j in range(2)]

            vec.memset(biasC[:], MAGIC)
            v_bc = tick(vec.memset(negC[:], -MAGIC), dve)
            nc.scalar.wait_ge(dve.h, v_bc)
            nc.gpsimd.wait_ge(dve.h, v_bc)

            batches = []
            for (w_in, scr, n_ch, width, sc_t) in [
                    (w_gate, gq, IC, D, sg_t), (w_up, uq, IC, D, su_t),
                    (w_down, dq, DC, I3, sd_t)]:
                w3d = w_in.rearrange("(c p) n -> p c n", p=128)
                s3d = scr.rearrange("(c p) n -> p c n", p=128)
                nh = WBW // width
                for c0 in range(0, n_ch, nh):
                    batches.append((w3d, s3d, c0, min(c0 + nh, n_ch), width, sc_t))
            NB = len(batches)

            vload = {}
            act_done = {}
            q_done = {}
            rs_done = {}
            cast_done = {}
            ln1_pe = {}

            def issue_load(i):
                w3d, _, c0, c1, width, _ = batches[i]
                if i >= 2:
                    if rs_done.get(i - 2) is not None:
                        nc.gpsimd.wait_ge(dve.h, rs_done[i - 2])
                    nc.gpsimd.wait_ge(act.h, act_done[i - 2])
                dst = wraw[i % 2][:, 0:(c1 - c0) * width].rearrange("p (c n) -> p c n", n=width)
                vload[i] = dma(nc.gpsimd, dst, w3d[:, c0:c1, :], wldp[i % 2])

            def emit_quant_batch(i):
                w3d, s3d, c0, c1, width, sc_t = batches[i]
                nb = c1 - c0
                wr = wraw[i % 2]
                rsvt = rsvt2[i % 2]
                if sc_t is not None:
                    vec.wait_ge(wldp[i % 2].h, vload[i])
                    if i >= 2 and rs_done.get(i - 2) is not None:
                        vec.wait_ge(act.h, act_done[i - 2])
                    last_rs = None
                    for c in range(nb):
                        wsl = wr[:, c * width:(c + 1) * width]
                        vec.tensor_reduce(out=rmax[:], in_=wsl, axis=AX.X, op=ALU.max,
                                          apply_absolute_value=True)
                        vec.tensor_scalar(out=sc_t[:, c0 + c:c0 + c + 1], in0=rmax[:],
                                          scalar1=1.0 / QMAX, scalar2=None, op0=ALU.mult)
                        last_rs = tick(vec.reciprocal(out=rsvt[:, c:c + 1],
                                                      in_=sc_t[:, c0 + c:c0 + c + 1]), dve)
                    rs_done[i] = last_rs
                    scl.wait_ge(dve.h, rs_done[i])
                    scl.wait_ge(wst.h, wst.v)
                    last_q = None
                    for c in range(nb):
                        wsl = wr[:, c * width:(c + 1) * width]
                        scl.activation(out=tq[c % 2][:, 0:width], in_=wsl,
                                       func=AF.Identity,
                                       scale=rsvt[:, c:c + 1], bias=biasC[:])
                        last_q = tick(scl.activation(out=qout[:, c * width:(c + 1) * width],
                                                     in_=tq[c % 2][:, 0:width], func=AF.Identity,
                                                     bias=negC[:]), act)
                    act_done[i] = last_q
                else:
                    rs_done[i] = None
                    scl.wait_ge(wldp[i % 2].h, vload[i])
                    scl.wait_ge(wst.h, wst.v)
                    last_q = None
                    for c in range(nb):
                        last_q = tick(scl.copy(out=qout[:, c * width:(c + 1) * width],
                                               in_=wr[:, c * width:(c + 1) * width]), act)
                        act_done[i] = last_q
                q_done[i] = last_q
                nc.scalar.wait_ge(act.h, last_q)
                dma(nc.scalar, s3d[:, c0:c1, :],
                    qout[:, 0:nb * width].rearrange("p (c n) -> p c n", n=width), wst)

            ln1_apply = {}
            xload1 = {}

            def issue_xload(t):
                j = t % 2
                if t >= 2:
                    nc.gpsimd.wait_ge(pe.h, ln1_pe[t - 2])
                    nc.gpsimd.wait_ge(dve.h, ln1_apply[t - 2])
                xload1[t] = dma(nc.gpsimd, x0b[j][:].rearrange("p (c n) -> p c n", n=T),
                                x3d[:, :, t * T:(t + 1) * T], sxg[j])

            def ln1_front(t):
                j = t % 2
                v_x = xload1[t]
                vec.wait_ge(sxg[j].h, v_x)
                ln1_pe[t] = ln_front(x0b[j], sqb[j], pstat1[j])

            def ln1_back(t):
                j = t % 2
                ln1_apply[t] = ln_back(x0b[j], pstat1[j], mrows2[j], xnsl(t), ln1_pe[t])
                tick(vec.tensor_reduce(out=amax8[0][:, t:t + 1], in_=xnsl(t),
                                       axis=AX.X, op=ALU.max,
                                       apply_absolute_value=True), dve)
                if t + 2 < NT:
                    issue_xload(t + 2)

            def emit_ln1_tile(t):
                ln1_front(t)
                if t > 0:
                    ln1_back(t - 1)
                if t == NT - 1:
                    ln1_back(t)

            issue_xload(0)
            issue_xload(1)
            issue_load(0)
            issue_load(1)
            nt_done = 0
            for i in range(NB):
                emit_quant_batch(i)
                if i + 2 < NB:
                    issue_load(i + 2)
                if nt_done < NT:
                    emit_ln1_tile(nt_done)
                    nt_done += 1
                if nt_done < NT and i >= NB - 2:
                    emit_ln1_tile(nt_done)
                    nt_done += 1
            while nt_done < NT:
                emit_ln1_tile(nt_done)
                nt_done += 1
            dma(nc.gpsimd, Wb, W_attn, qg)

        def gu_phase(call, wgu_sb, v_wready, amc, svec, v_scales):
            with ExitStack() as gctx:
                stag = gctx.enter_context(nc.sbuf_tensor(f"stag{call}", [128, IC * T], BF16))
                tscrq = gctx.enter_context(nc.sbuf_tensor(f"tscrq{call}", [128, 1024], F32))
                qx_ready = {}
                quant_prev = [None]

                def quant_quarter_x(tt, qq):
                    sl = XN["t"][:, tt * (DC * T) + qq * 1024:tt * (DC * T) + (qq + 1) * 1024]
                    if quant_prev[0] is not None:
                        scl.wait_ge(dve.h, quant_prev[0])
                    va = tick(scl.activation(out=tscrq[:], in_=sl, func=AF.Identity,
                                             scale=svec[:, 1:2], bias=biasC[:]), act)
                    vec.wait_ge(act.h, va)
                    vec.tensor_scalar(out=tscrq[:], in0=tscrq[:], scalar1=MAGIC - 128.0,
                                      scalar2=MAGIC + 127.0, op0=ALU.max, op1=ALU.min)
                    v = tick(vec.tensor_scalar(out=sl, in0=tscrq[:], scalar1=MAGIC,
                                               scalar2=None, op0=ALU.subtract), dve)
                    quant_prev[0] = v
                    if qq == 3:
                        qx_ready[tt] = v
                NR = 4
                sgl = [gctx.enter_context(nc.sbuf_tensor(f"sgl{call}_{j}", [128, T], BF16)) for j in range(NR)]
                utl = [gctx.enter_context(nc.sbuf_tensor(f"utl{call}_{j}", [128, T], BF16)) for j in range(NR)]
                pg = [gctx.enter_context(nc.psum_tensor(f"pg{call}_{j}", [128, T], F32)) for j in range(NR)]
                pu = [gctx.enter_context(nc.psum_tensor(f"pu{call}_{j}", [128, T], F32)) for j in range(NR)]
                nc.tensor.wait_ge(wgu.h, v_wready)
                scl.wait_ge(dve.h, v_scales)
                for qq in range(4):
                    quant_quarter_x(0, qq)
                silu_done = {}
                h_done = {}
                st_half = {}
                HSPL = 11
                it = 0
                for t in range(NT):
                    nc.tensor.wait_ge(dve.h, qx_ready[t])
                    for c in range(IC):
                        if t + 1 < NT and c in (3, 7, 11, 15):
                            quant_quarter_x(t + 1, (c - 3) // 4)
                        j = it % NR
                        if it >= NR:
                            nc.tensor.wait_ge(act.h, silu_done[it - NR])
                            nc.tensor.wait_ge(dve.h, h_done[it - NR])
                        for k in range(DC):
                            mmg = nc.tensor.matmul(pg[j][:], wgu_sb[:, k * I3 + 128 * c:k * I3 + 128 * (c + 1)],
                                                   XN["t"][:, t * (DC * T) + k * T:t * (DC * T) + (k + 1) * T],
                                                   start=(k == 0), stop=(k == DC - 1))
                        vg = tick(mmg, pe)
                        for k in range(DC):
                            mmu = nc.tensor.matmul(pu[j][:], wgu_sb[:, (8 + k) * I3 + 128 * c:(8 + k) * I3 + 128 * (c + 1)],
                                                   XN["t"][:, t * (DC * T) + k * T:t * (DC * T) + (k + 1) * T],
                                                   start=(k == 0), stop=(k == DC - 1))
                        vu = tick(mmu, pe)
                        scl.wait_ge(pe.h, vg)
                        if it >= NR:
                            scl.wait_ge(dve.h, h_done[it - NR])
                        silu_done[it] = tick(scl.activation(
                            out=sgl[j][:], in_=pg[j][:], func=AF.Silu,
                            scale=sgc[call][:, c:c + 1]), act)
                        vec.wait_ge(pe.h, vu)
                        vec.tensor_scalar(out=utl[j][:], in0=pu[j][:],
                                                scalar1=suc[call][:, c:c + 1], scalar2=None,
                                                op0=ALU.mult)
                        vec.wait_ge(act.h, silu_done[it])
                        if t > 0 and c in (0, HSPL):
                            half = 0 if c == 0 else 1
                            vec.wait_ge(shstp[half].h, st_half[(t - 1, half)])
                        h_done[it] = tick(vec.tensor_tensor(
                            out=stag[:, c * T:(c + 1) * T], in0=sgl[j][:], in1=utl[j][:],
                            op=ALU.mult), dve)
                        tick(vec.tensor_reduce(out=amc[:, t * IC + c:t * IC + c + 1],
                                               in_=stag[:, c * T:(c + 1) * T],
                                               axis=AX.X, op=ALU.max,
                                               apply_absolute_value=True), dve)
                        if c == HSPL - 1:
                            nc.sync.wait_ge(dve.h, h_done[it])
                            st_half[(t, 0)] = dma(nc.sync, h_scr[128 * t:128 * (t + 1), 0:HSPL * T],
                                                  stag[:, 0:HSPL * T], shstp[0])
                        it += 1
                    nc.sync.wait_ge(dve.h, h_done[it - 1])
                    st_half[(t, 1)] = dma(nc.sync, h_scr[128 * t:128 * (t + 1), HSPL * T:],
                                          stag[:, HSPL * T:], shstp[1])
                nc.sync.wait_ge(shstp[0].h, st_half[(NT - 1, 0)])
                nc.sync.wait_ge(shstp[1].h, st_half[(NT - 1, 1)])
            return

        def load_wgu(call, octx):
            wgu_sb = octx.enter_context(nc.sbuf_tensor(f"wgu{call}", [128, 16 * I3], BF16))
            v = 0
            for k in range(DC):
                v = dma(nc.scalar, wgu_sb[:, k * I3:(k + 1) * I3],
                        gq[:, 128 * k:128 * (k + 1)], wgu, transpose=True)
            for k in range(DC):
                v = dma(nc.scalar, wgu_sb[:, (8 + k) * I3:(9 + k) * I3],
                        uq[:, 128 * k:128 * (k + 1)], wgu, transpose=True)
            return wgu_sb, v

        def down_phase(call):
            first = (call == 0)
            QF = IC * T // 4
            with ExitStack() as dctx:
                qdT = dctx.enter_context(nc.sbuf_tensor(f"qdT{call}", [128, IC * D], BF16))
                hbuf = dctx.enter_context(nc.sbuf_tensor(f"hbuf{call}", [128, IC * T], BF16))
                qh = [dctx.enter_context(nc.sbuf_tensor(f"qh{call}_{j}", [128, IC * T], BF16)) for j in range(2)]
                tscr = dctx.enter_context(nc.sbuf_tensor(f"tscr{call}", [128, QF], F32))
                xres = dctx.enter_context(nc.sbuf_tensor(f"xres{call}", [128, TOK], F32 if first else BF16))
                nxout = 2 if first else 1
                xout = [dctx.enter_context(nc.sbuf_tensor(f"xout{call}_{j}", [128, TOK],
                                                          BF16 if first else F32)) for j in range(nxout)]
                pd = [dctx.enter_context(nc.psum_tensor(f"pd{call}_{j}", [128, T], F32)) for j in range(4)]
                if first:
                    sqb = [dctx.enter_context(nc.sbuf_tensor(f"sqb2_{j}", [128, TOK], BF16)) for j in range(2)]
                    m_t = dctx.enter_context(nc.sbuf_tensor("m_t2", [128, T], F32))
                    var_t = dctx.enter_context(nc.sbuf_tensor("var_t2", [128, T], F32))
                    rstd_t = dctx.enter_context(nc.sbuf_tensor("rstd_t2", [128, T], F32))
                    mb = dctx.enter_context(nc.sbuf_tensor("mb2", [128, T], BF16))
                    rstdb = dctx.enter_context(nc.sbuf_tensor("rstdb2", [128, T], BF16))
                    t1b = dctx.enter_context(nc.sbuf_tensor("t1b2", [128, T], BF16))
                    xn2st = dctx.enter_context(nc.sbuf_tensor("xn2st", [128, TOK], BF16))
                    pstat = [dctx.enter_context(nc.psum_tensor(f"pstat2_{j}", [128, 2 * T], F32)) for j in range(2)]
                v_wd = 0
                for k in range(IC):
                    v_wd = dma(nc.scalar, qdT[:, k * D:(k + 1) * D],
                               dq[:, 128 * k:128 * (k + 1)], qd, transpose=True)
                svec = sv[1] if first else sv[3]
                scomb = sdc[call]
                qh_ready = {}
                mm_done = {}
                stt_done = {}
                hload = {}
                xload = {}
                out_store = {}
                xn2_store = {}
                ln_pe = {}
                v_sq = {}
                evac_dve = {}
                nc.tensor.wait_ge(qd.h, v_wd)
                hload[0] = dma(nc.sync, hbuf[:], h_scr[0:128, :], shl)
                if first:
                    xload[0] = dma(nc.sync, xres[:].rearrange("p (c n) -> p c n", n=T),
                                   x3d[:, :, 0:T], sx)
                else:
                    xload[0] = dma(nc.sync, xres[:], x2_scr[0:128, :], sx)

                qprev = [None]

                def quant_quarter(tt, q):
                    sl_in = hbuf[:, q * QF:(q + 1) * QF]
                    sl_out = qh[tt % 2][:, q * QF:(q + 1) * QF]
                    if qprev[0] is not None:
                        scl.wait_ge(dve.h, qprev[0])
                    va = tick(scl.activation(out=tscr[:], in_=sl_in, func=AF.Identity,
                                             scale=svec[:, 1:2], bias=biasC[:]), act)
                    vec.wait_ge(act.h, va)
                    vec.tensor_scalar(out=tscr[:], in0=tscr[:], scalar1=MAGIC - 128.0,
                                      scalar2=MAGIC + 127.0, op0=ALU.max, op1=ALU.min)
                    v = tick(vec.tensor_scalar(out=sl_out, in0=tscr[:], scalar1=MAGIC,
                                               scalar2=None, op0=ALU.subtract), dve)
                    qprev[0] = v
                    if q == 3:
                        qh_ready[tt] = v

                def ln2_rest(tt):
                    vec.wait_ge(pe.h, ln_pe[tt])
                    ps = pstat[tt % 2]
                    vec.tensor_scalar(out=m_t[:], in0=ps[:, 0:T], scalar1=1.0 / D,
                                      scalar2=None, op0=ALU.mult)
                    vec.tensor_scalar(out=var_t[:], in0=ps[:, T:2 * T], scalar1=1.0 / D,
                                      scalar2=EPS, op0=ALU.mult, op1=ALU.add)
                    vec.tensor_tensor(out=rstd_t[:], in0=m_t[:], in1=m_t[:], op=ALU.mult)
                    v_var = tick(vec.tensor_tensor(out=var_t[:], in0=var_t[:], in1=rstd_t[:],
                                                   op=ALU.subtract), dve)
                    scl.wait_ge(dve.h, v_var)
                    scl.activation(out=rstd_t[:], in_=var_t[:], func=AF.Ln)
                    v_r = tick(scl.activation(out=rstd_t[:], in_=rstd_t[:], func=AF.Exp,
                                              scale=-0.5), act)
                    vec.wait_ge(act.h, v_r)
                    if tt > 0:
                        vec.wait_ge(sxn2.h, xn2_store[tt - 1])
                    vec.tensor_copy(out=mb[:], in_=m_t[:])
                    vec.tensor_copy(out=rstdb[:], in_=rstd_t[:])
                    xo3 = xn2st[:].rearrange("p (c n) -> p c n", n=T)
                    mbb = mb[:].unsqueeze(1).broadcast_to([128, DC, T])
                    rbb = rstdb[:].unsqueeze(1).broadcast_to([128, DC, T])
                    vec.tensor_tensor(out=xo3, in0=xout[tt % 2][:].rearrange("p (c n) -> p c n", n=T),
                                      in1=mbb, op=ALU.subtract)
                    last = tick(vec.tensor_tensor(out=xo3, in0=xo3, in1=rbb, op=ALU.mult), dve)
                    evac_dve[tt] = last
                    nc.scalar.wait_ge(dve.h, last)
                    xn2_store[tt] = dma(nc.scalar, xn2_scr[128 * tt:128 * (tt + 1), :],
                                        xn2st[:], sxn2)

                vec.wait_ge(shl.h, hload[0])
                scl.wait_ge(shl.h, hload[0])
                scl.wait_ge(dve.h, dve.v)
                for q in range(4):
                    quant_quarter(0, q)
                nc.sync.wait_ge(dve.h, qh_ready[0])
                if NT > 1:
                    hload[1] = dma(nc.sync, hbuf[:], h_scr[128:256, :], shl)

                for t in range(NT):
                    xo = xout[t % nxout]
                    nc.tensor.wait_ge(dve.h, qh_ready[t])
                    for m in range(DC):
                        j = m % 4
                        if m >= 4:
                            nc.tensor.wait_ge(dve.h, stt_done[(t, m - 4)])
                        elif t > 0:
                            nc.tensor.wait_ge(dve.h, stt_done[(t - 1, m + 4)])
                        for k in range(IC):
                            mmd = nc.tensor.matmul(pd[j][:], qdT[:, k * D + 128 * m:k * D + 128 * (m + 1)],
                                                   qh[t % 2][:, k * T:(k + 1) * T],
                                                   start=(k == 0), stop=(k == IC - 1))
                        mm_done[(t, m)] = tick(mmd, pe)
                        if first and m == 1 and t > 0:
                            tp = t - 1
                            nc.tensor.wait_ge(dve.h, v_sq[tp])
                            if tp >= 2:
                                nc.tensor.wait_ge(dve.h, evac_dve[tp - 2])
                            ps = pstat[tp % 2]
                            for c in range(DC):
                                nc.tensor.matmul(ps[:, 0:T], ones128[:],
                                                 xout[tp % 2][:, c * T:(c + 1) * T],
                                                 start=(c == 0), stop=(c == DC - 1))
                            for c in range(DC):
                                mms = nc.tensor.matmul(ps[:, T:2 * T], ones128[:],
                                                       sqb[tp % 2][:, c * T:(c + 1) * T],
                                                       start=(c == 0), stop=(c == DC - 1))
                            ln_pe[tp] = tick(mms, pe)
                        vec.wait_ge(pe.h, mm_done[(t, m)])
                        if m == 0:
                            vec.wait_ge(sx.h, xload[t])
                            if t >= nxout:
                                if first:
                                    vec.wait_ge(sx1p[t % 2].h, out_store[t - nxout])
                                else:
                                    vec.wait_ge(sout.h, out_store[t - nxout])
                        stt_done[(t, m)] = tick(vec.scalar_tensor_tensor(
                            out=xo[:, m * T:(m + 1) * T], in0=pd[j][:],
                            scalar=scomb[:, m:m + 1], in1=xres[:, m * T:(m + 1) * T],
                            op0=ALU.mult, op1=ALU.add), dve)
                        if m % 2 == 1 and t + 1 < NT:
                            q = m // 2
                            if q == 0:
                                scl.wait_ge(shl.h, hload[t + 1])
                                vec.wait_ge(shl.h, hload[t + 1])
                                if t >= 1:
                                    vec.wait_ge(pe.h, mm_done[(t - 1, DC - 1)])
                            quant_quarter(t + 1, q)
                    if t + 1 < NT:
                        nc.sync.wait_ge(dve.h, stt_done[(t, DC - 1)])
                        if first:
                            xload[t + 1] = dma(nc.sync, xres[:].rearrange("p (c n) -> p c n", n=T),
                                               x3d[:, :, (t + 1) * T:(t + 2) * T], sx)
                        else:
                            xload[t + 1] = dma(nc.sync, xres[:],
                                               x2_scr[128 * (t + 1):128 * (t + 2), :], sx)
                    if t + 2 < NT:
                        nc.sync.wait_ge(dve.h, qh_ready[t + 1])
                        hload[t + 2] = dma(nc.sync, hbuf[:], h_scr[128 * (t + 2):128 * (t + 3), :], shl)
                    nc.scalar.wait_ge(dve.h, stt_done[(t, DC - 1)])
                    if first:
                        out_store[t] = dma(nc.scalar, x1_scr[128 * t:128 * (t + 1), :], xo[:], sx1p[t % 2])
                        v_sq[t] = tick(vec.tensor_tensor(out=sqb[t % 2][:], in0=xo[:],
                                                         in1=xo[:], op=ALU.mult), dve)
                        if t > 0:
                            ln2_rest(t - 1)
                    else:
                        out_store[t] = dma(nc.scalar, o3d[:, :, t * T:(t + 1) * T],
                                           xo[:].rearrange("p (c n) -> p c n", n=T), sout)
                snap_end = fence_snapshot()
                if first:
                    tp = NT - 1
                    nc.tensor.wait_ge(dve.h, v_sq[tp])
                    nc.tensor.wait_ge(dve.h, evac_dve[tp - 2])
                    ps = pstat[tp % 2]
                    for c in range(DC):
                        nc.tensor.matmul(ps[:, 0:T], ones128[:],
                                         xout[tp % 2][:, c * T:(c + 1) * T],
                                         start=(c == 0), stop=(c == DC - 1))
                    for c in range(DC):
                        mms = nc.tensor.matmul(ps[:, T:2 * T], ones128[:],
                                               sqb[tp % 2][:, c * T:(c + 1) * T],
                                               start=(c == 0), stop=(c == DC - 1))
                    ln_pe[tp] = tick(mms, pe)
                    ln2_rest(tp)
            return snap_end, xn2_store

        barrier()
        with ExitStack() as octx0:
            wgu_sb0, v_wr0 = load_wgu(0, octx0)
            global_max_barrier(0, amax8[0], NT)
            vec.wait_ge(act.h, v_sig)
            vec.tensor_scalar(out=sgc[0][:], in0=sg_t[:], scalar1=sv[0][:, 0:1],
                              scalar2=None, op0=ALU.mult)
            v_sc0 = tick(vec.tensor_scalar(out=suc[0][:], in0=su_t[:], scalar1=sv[0][:, 0:1],
                                           scalar2=None, op0=ALU.mult), dve)
            gu_phase(0, wgu_sb0, v_wr0, amax8[1], sv[0], v_sc0)
        seg1.close()
        pre1 = fence_snapshot()
        global_max_barrier(1, amax8[1], NT * IC)
        tick(vec.tensor_scalar(out=sdc[0][:], in0=sd_t[:], scalar1=sv[1][:, 0:1],
                                     scalar2=0.5, op0=ALU.mult, op1=ALU.mult), dve)
        queue_fence(pre1)
        snap_d0, xn2_store_d0 = down_phase(0)
        queue_fence(snap_d0)
        for s_ in (pe, dve, act, cc):
            vec.wait_ge(s_.h, s_.v)
            scl.wait_ge(s_.h, s_.v)
            nc.tensor.wait_ge(s_.h, s_.v)
            nc.gpsimd.wait_ge(s_.h, s_.v)

        with ExitStack() as actx:
            Wbt = actx.enter_context(nc.sbuf_tensor("Wbt", [128, DC * H3], BF16))
            xn2b = [actx.enter_context(nc.sbuf_tensor(f"xn2b{j}", [128, TOK], BF16)) for j in range(2)]
            ig_t = actx.enter_context(nc.sbuf_tensor("ig_t", [128, TOK], BF16))
            f_t = actx.enter_context(nc.sbuf_tensor("f_t", [128, TOK], BF16))
            th_t = actx.enter_context(nc.sbuf_tensor("th_t", [128, TOK], BF16))
            u_t = actx.enter_context(nc.sbuf_tensor("u_t", [128, TOK], BF16))
            og_st = actx.enter_context(nc.sbuf_tensor("og_st", [128, TOK], BF16))
            hl_st = actx.enter_context(nc.sbuf_tensor("hl_st", [128, TOK], BF16))
            pp_st = actx.enter_context(nc.sbuf_tensor("pp_st", [128, TOK], BF16))
            pa = [actx.enter_context(nc.psum_tensor(f"pa{j}", [128, T], F32)) for j in range(4)]
            pcar = actx.enter_context(nc.psum_tensor("pcar", [8, 128], F32))
            pcar2 = actx.enter_context(nc.psum_tensor("pcar2", [128, 8], F32))
            v_wb = 0
            for k in range(DC):
                v_wb = dma(nc.scalar, Wbt[:, k * H3:(k + 1) * H3],
                           Wb[128 * k:128 * (k + 1), :], wb)
            nc.sync.wait_ge(sxn2.h, xn2_store_d0[0])
            xl = {0: dma(nc.sync, xn2b[0][:], xn2_scr[0:128, :], sxl[0])}
            evac_done = {}
            scan_done = {}
            u_done = {}
            st_done = {}
            mm_done = {}
            nc.tensor.wait_ge(wb.h, v_wb)
            for t in range(NT):
                if t + 1 < NT:
                    if t >= 1:
                        nc.sync.wait_ge(pe.h, mm_done[(t - 1, 23)])
                    nc.sync.wait_ge(sxn2.h, xn2_store_d0[t + 1])
                    xl[t + 1] = dma(nc.sync, xn2b[(t + 1) % 2][:],
                                    xn2_scr[128 * (t + 1):128 * (t + 2), :], sxl[(t + 1) % 2])
                nc.tensor.wait_ge(sxl[t % 2].h, xl[t])
                mm_order = list(range(8, 16)) + list(range(0, 8)) + list(range(16, 24))
                for mi, m in enumerate(mm_order):
                    j = mi % 4
                    if mi >= 4:
                        nc.tensor.wait_ge(act.h, evac_done[(t, mm_order[mi - 4])])
                    elif t > 0:
                        nc.tensor.wait_ge(act.h, evac_done[(t - 1, mm_order[20 + mi])])
                    for k in range(DC):
                        mma = nc.tensor.matmul(pa[j][:], Wbt[:, k * H3 + 128 * m:k * H3 + 128 * (m + 1)],
                                               xn2b[t % 2][:, k * T:(k + 1) * T],
                                               start=(k == 0), stop=(k == DC - 1))
                    mm_done[(t, m)] = tick(mma, pe)
                    c = m % 8
                    csl = slice(c * T, (c + 1) * T)
                    scl.wait_ge(pe.h, mm_done[(t, m)])
                    if m >= 8 and m < 16:
                        if t > 0:
                            scl.wait_ge(dve.h, scan_done[(t - 1, c)])
                        scl.activation(out=ig_t[:, csl], in_=pa[j][:], func=AF.Sigmoid,
                                             bias=bias_t[:, m:m + 1])
                        evac_done[(t, m)] = tick(scl.activation(
                            out=f_t[:, csl], in_=pa[j][:], func=AF.Sigmoid,
                            scale=-1.0, bias=negb_t[:, m:m + 1]), act)
                    elif m < 8:
                        if t > 0:
                            scl.wait_ge(dve.h, u_done[(t - 1, c)])
                        evac_done[(t, m)] = tick(scl.activation(
                            out=th_t[:, csl], in_=pa[j][:], func=AF.Tanh,
                            bias=bias_t[:, m:m + 1]), act)
                        vec.wait_ge(act.h, evac_done[(t, m)])
                        u_done[(t, c)] = tick(vec.tensor_tensor(
                            out=u_t[:, csl], in0=th_t[:, csl], in1=ig_t[:, csl],
                            op=ALU.mult), dve)
                    else:
                        if t > 0:
                            scl.wait_ge(sast.h, st_done[t - 1])
                        evac_done[(t, m)] = tick(scl.activation(
                            out=og_st[:, csl], in_=pa[j][:], func=AF.Sigmoid,
                            bias=bias_t[:, m:m + 1]), act)
                for c in range(DC):
                    csl = slice(c * T, (c + 1) * T)
                    vec.wait_ge(act.h, evac_done[(t, 8 + c)])
                    if t > 0:
                        vec.wait_ge(sast.h, st_done[t - 1])
                    vec.tensor_tensor_scan(out=hl_st[:, csl], data0=f_t[:, csl],
                                                 data1=u_t[:, csl],
                                                 initial=state_h[:, c:c + 1],
                                                 op0=ALU.mult, op1=ALU.add)
                    vec.tensor_copy(out=state_h[:, c:c + 1], in_=hl_st[:, c * T + T - 1:c * T + T])
                    vec.tensor_tensor_scan(out=pp_st[:, csl], data0=f_t[:, csl],
                                                 data1=zerosT[:],
                                                 initial=state_p[:, c:c + 1],
                                                 op0=ALU.mult, op1=ALU.add)
                    scan_done[(t, c)] = tick(vec.tensor_copy(
                        out=state_p[:, c:c + 1], in_=pp_st[:, c * T + T - 1:c * T + T]), dve)
                nc.sync.wait_ge(dve.h, scan_done[(t, DC - 1)])
                nc.sync.wait_ge(act.h, evac_done[(t, 23)])
                dma(nc.sync, hl_scr[128 * t:128 * (t + 1), :], hl_st[:], sast)
                dma(nc.sync, pp_scr[128 * t:128 * (t + 1), :], pp_st[:], sast)
                st_done[t] = dma(nc.sync, og_scr[128 * t:128 * (t + 1), :], og_st[:], sast)

            pre_carry = fence_snapshot()
            v_c0 = tick(vec.tensor_scalar(out=h_in_t[:], in0=state_h[:],
                                                scalar1=flv[:, 0:1], scalar2=None,
                                                op0=ALU.mult), dve)
            nc.tensor.wait_ge(dve.h, v_c0)
            v_c1 = tick(nc.tensor.transpose(pcar[:], h_in_t[:], identf[:]), pe)
            vec.wait_ge(pe.h, v_c1)
            v_c2 = tick(vec.tensor_copy(out=carr_sb[:], in_=pcar[:]), dve)
            nc.gpsimd.wait_ge(dve.h, v_c2)
            v_c3 = dma(nc.gpsimd, carry_i, carr_sb[:], qg)
            nc.gpsimd.wait_ge(qg.h, v_c3)
            v_c4 = tick(nc.gpsimd.collective_compute(
                "AllReduce", ALU.add, replica_groups=[[0, 1], [2, 3], [4, 5], [6, 7]],
                ins=[carry_i], outs=[carry_o]), cc)
            nc.gpsimd.wait_ge(cc.h, v_c4)
            v_c5 = dma(nc.gpsimd, carr_sb[:], carry_o, qg)
            nc.tensor.wait_ge(qg.h, v_c5)
            v_c6 = tick(nc.tensor.transpose(pcar2[:], carr_sb[:], identf[0:8, 0:8]), pe)
            vec.wait_ge(pe.h, v_c6)
            tick(vec.tensor_scalar(out=h_in_t[:], in0=pcar2[:], scalar1=flv[:, 1:2],
                                         scalar2=None, op0=ALU.mult), dve)
        queue_fence(pre_carry)

        seg2 = ExitStack()
        XN["t"] = seg2.enter_context(nc.sbuf_tensor("xn_seg2", [128, DC * TOK], BF16))
        with ExitStack() as bctx:
            hlb = [bctx.enter_context(nc.sbuf_tensor(f"hlb{j}", [128, TOK], BF16)) for j in range(2)]
            ppb = [bctx.enter_context(nc.sbuf_tensor(f"ppb{j}", [128, TOK], BF16)) for j in range(2)]
            ogb = [bctx.enter_context(nc.sbuf_tensor(f"ogb{j}", [128, TOK], BF16)) for j in range(2)]
            x1b = [bctx.enter_context(nc.sbuf_tensor(f"x1b{j}", [128, TOK], BF16)) for j in range(2)]
            htr = bctx.enter_context(nc.sbuf_tensor("htr", [128, TOK], BF16))
            thb = bctx.enter_context(nc.sbuf_tensor("thb", [128, TOK], BF16))
            x2st = [bctx.enter_context(nc.sbuf_tensor(f"x2st{j}", [128, TOK], BF16)) for j in range(2)]
            sqb3 = [bctx.enter_context(nc.sbuf_tensor(f"sqb3_{j}", [128, TOK], BF16)) for j in range(2)]
            m_t3 = bctx.enter_context(nc.sbuf_tensor("m_t3", [128, T], F32))
            var3 = bctx.enter_context(nc.sbuf_tensor("var3", [128, T], F32))
            rstd3 = bctx.enter_context(nc.sbuf_tensor("rstd3", [128, T], F32))
            mb3 = bctx.enter_context(nc.sbuf_tensor("mb3", [128, T], BF16))
            rstdb3 = bctx.enter_context(nc.sbuf_tensor("rstdb3", [128, T], BF16))
            pstat3 = [bctx.enter_context(nc.psum_tensor(f"pstat3_{j}", [128, 2 * T], F32))
                      for j in range(2)]
            lds = {}
            x2_done = {}
            stats_pe = {}
            ln3_dve = {}
            for t in range(2):
                lds[t] = [dma(nc.sync, hlb[t][:], hl_scr[128 * t:128 * (t + 1), :], spb[t]),
                          dma(nc.sync, ppb[t][:], pp_scr[128 * t:128 * (t + 1), :], spb[t]),
                          dma(nc.sync, ogb[t][:], og_scr[128 * t:128 * (t + 1), :], spb[t]),
                          dma(nc.sync, x1b[t][:], x1_scr[128 * t:128 * (t + 1), :], spb[t])]
            def pb_front(t):
                j = t % 2
                vec.wait_ge(spb[j].h, lds[t][-1])
                if t > 1:
                    vec.wait_ge(pe.h, stats_pe[t - 2])
                    vec.wait_ge(sx2p[j].h, x2_done[t - 2])
                htr3 = htr[:].rearrange("p (c n) -> p c n", n=T)
                hinb = h_in_t[:].unsqueeze(2).broadcast_to([128, DC, T])
                vec.tensor_tensor(out=htr3, in0=ppb[j][:].rearrange("p (c n) -> p c n", n=T),
                                  in1=hinb, op=ALU.mult)
                v_h = tick(vec.tensor_tensor(out=htr[:], in0=htr[:], in1=hlb[j][:],
                                             op=ALU.add), dve)
                scl.wait_ge(dve.h, v_h)
                v_th = tick(scl.activation(out=thb[:], in_=htr[:], func=AF.Tanh), act)
                if t > 0:
                    pb_back(t - 1)
                vec.wait_ge(act.h, v_th)
                vec.tensor_tensor(out=thb[:], in0=thb[:], in1=ogb[j][:], op=ALU.mult)
                tick(vec.tensor_tensor(out=x2st[j][:], in0=thb[:], in1=x1b[j][:],
                                       op=ALU.add), dve)
                v_sq = tick(vec.tensor_tensor(out=sqb3[j][:], in0=x2st[j][:],
                                              in1=x2st[j][:], op=ALU.mult), dve)
                v_sq_t[t] = v_sq
                nc.sync.wait_ge(dve.h, v_sq)
                x2_done[t] = dma(nc.sync, x2_scr[128 * t:128 * (t + 1), :], x2st[j][:], sx2p[j])
                if t + 2 < NT:
                    tt = t + 2
                    lds[tt] = [dma(nc.sync, hlb[j][:], hl_scr[128 * tt:128 * (tt + 1), :], spb[j]),
                               dma(nc.sync, ppb[j][:], pp_scr[128 * tt:128 * (tt + 1), :], spb[j]),
                               dma(nc.sync, ogb[j][:], og_scr[128 * tt:128 * (tt + 1), :], spb[j]),
                               dma(nc.sync, x1b[j][:], x1_scr[128 * tt:128 * (tt + 1), :], spb[j])]
                nc.tensor.wait_ge(dve.h, v_sq)
                if t > 1:
                    nc.tensor.wait_ge(dve.h, ln3_dve[t - 2])
                for c in range(DC):
                    nc.tensor.matmul(pstat3[j][:, 0:T], ones128[:], x2st[j][:, c * T:(c + 1) * T],
                                     start=(c == 0), stop=(c == DC - 1))
                for c in range(DC):
                    mm3 = nc.tensor.matmul(pstat3[j][:, T:2 * T], ones128[:],
                                           sqb3[j][:, c * T:(c + 1) * T],
                                           start=(c == 0), stop=(c == DC - 1))
                stats_pe[t] = tick(mm3, pe)

            def pb_back(t):
                j = t % 2
                vec.wait_ge(pe.h, stats_pe[t])
                vec.tensor_scalar(out=m_t3[:], in0=pstat3[j][:, 0:T], scalar1=1.0 / D,
                                  scalar2=None, op0=ALU.mult)
                vec.tensor_scalar(out=var3[:], in0=pstat3[j][:, T:2 * T], scalar1=1.0 / D,
                                  scalar2=EPS, op0=ALU.mult, op1=ALU.add)
                vec.tensor_tensor(out=rstd3[:], in0=m_t3[:], in1=m_t3[:], op=ALU.mult)
                v_v = tick(vec.tensor_tensor(out=var3[:], in0=var3[:], in1=rstd3[:],
                                             op=ALU.subtract), dve)
                scl.wait_ge(dve.h, v_v)
                scl.activation(out=rstd3[:], in_=var3[:], func=AF.Ln)
                v_r = tick(scl.activation(out=rstd3[:], in_=rstd3[:], func=AF.Exp,
                                          scale=-0.5), act)
                vec.wait_ge(act.h, v_r)
                vec.tensor_copy(out=mb3[:], in_=m_t3[:])
                vec.tensor_copy(out=rstdb3[:], in_=rstd3[:])
                xo3 = xnsl(t).rearrange("p (c n) -> p c n", n=T)
                mbb = mb3[:].unsqueeze(1).broadcast_to([128, DC, T])
                rbb = rstdb3[:].unsqueeze(1).broadcast_to([128, DC, T])
                vec.tensor_tensor(out=xo3, in0=x2st[j][:].rearrange("p (c n) -> p c n", n=T),
                                  in1=mbb, op=ALU.subtract)
                ln3_dve[t] = tick(vec.tensor_tensor(out=xo3, in0=xo3, in1=rbb,
                                                    op=ALU.mult), dve)
                tick(vec.tensor_reduce(out=amax8[2][:, t:t + 1], in_=xnsl(t),
                                       axis=AX.X, op=ALU.max,
                                       apply_absolute_value=True), dve)

            v_sq_t = {}
            for t in range(NT):
                pb_front(t)
            pb_back(NT - 1)

        barrier()

        with ExitStack() as octx1:
            wgu_sb1, v_wr1 = load_wgu(1, octx1)
            global_max_barrier(2, amax8[2], NT)
            vec.tensor_scalar(out=sgc[1][:], in0=sg_t[:], scalar1=sv[2][:, 0:1],
                              scalar2=None, op0=ALU.mult)
            v_sc1 = tick(vec.tensor_scalar(out=suc[1][:], in0=su_t[:], scalar1=sv[2][:, 0:1],
                                           scalar2=None, op0=ALU.mult), dve)
            gu_phase(1, wgu_sb1, v_wr1, amax8[3], sv[2], v_sc1)
        seg2.close()
        pre2 = fence_snapshot()
        global_max_barrier(3, amax8[3], NT * IC)
        tick(vec.tensor_scalar(out=sdc[1][:], in0=sd_t[:], scalar1=sv[3][:, 0:1],
                                     scalar2=0.5, op0=ALU.mult, op1=ALU.mult), dve)
        queue_fence(pre2)
        down_phase(1)
        barrier()

    return nc


_NC_CACHE = {}


def kernel(L0_attn_x, L0_attn_W, L0_attn_b, L0_ffn_clip_x, L0_ffn_clip_down,
           L0_ffn_w_gate, L0_ffn_w_up, L0_ffn_w_down):
    x = np.asarray(L0_attn_x, dtype=np.float32)
    W = np.ascontiguousarray(np.asarray(L0_attn_W, dtype=np.float32))
    b = np.ascontiguousarray(np.asarray(L0_attn_b, dtype=np.float32).reshape(PC, 128))
    wg = np.ascontiguousarray(np.asarray(L0_ffn_w_gate, dtype=np.float32))
    wu = np.ascontiguousarray(np.asarray(L0_ffn_w_up, dtype=np.float32))
    wd = np.ascontiguousarray(np.asarray(L0_ffn_w_down, dtype=np.float32))
    cx = float(np.asarray(L0_ffn_clip_x).reshape(()))
    cd = float(np.asarray(L0_ffn_clip_down).reshape(()))
    clips = np.array([[cx, cd]], dtype=np.float32)
    ident = np.eye(128, dtype=np.float32)

    if "nc" not in _NC_CACHE:
        _NC_CACHE["nc"] = build()
    nc = _NC_CACHE["nc"]

    in_maps = []
    for c in range(8):
        bb, k = c // 2, c % 2
        xs_ = np.ascontiguousarray(x[bb, TOK * k:TOK * (k + 1), :].T)
        in_maps.append({
            "x_fm": xs_, "w_gate": wg, "w_up": wu, "w_down": wd,
            "W_attn": W, "b_attn": b, "clips": clips,
            "flags": np.array([[1.0 - k, float(k)]], dtype=np.float32),
            "identin": ident,
        })
    res = run_bass_kernel_spmd(nc, in_maps, core_ids=list(range(8)))
    out = np.empty((4, 2 * TOK, D), dtype=np.float32)
    for c in range(8):
        bb, k = c // 2, c % 2
        out[bb, TOK * k:TOK * (k + 1), :] = np.asarray(res.results[c]["out_fm"]).T
    return out


if __name__ == "__main__":
    nc = build()
    n = sum(1 for _ in nc.all_instructions())
    print("built ok, instructions:", n)


# version 20
# speedup vs baseline: 1.0229x; 1.0229x over previous
import numpy as np
import ml_dtypes
from contextlib import ExitStack

import concourse.bass as bass
import concourse.mybir as mybir
from concourse.bass_utils import run_bass_kernel_spmd

F32 = mybir.dt.float32
BF16 = mybir.dt.bfloat16
ALU = mybir.AluOpType
AF = mybir.ActivationFunctionType
AX = mybir.AxisListType

MAGIC = 12582912.0
QMAX = 127.0
D, I3, H3 = 1024, 2816, 3072
DC, IC, PC = 8, 22, 24
NT, T, TOK = 8, 512, 4096
EPS = 1e-5


class Sem:
    def __init__(self, ctx, nc, name):
        self.h = ctx.enter_context(nc.semaphore(name))
        self.v = 0


class DrainingEngine:
    OPS = frozenset([
        "tensor_tensor", "tensor_scalar", "tensor_copy", "tensor_reduce",
        "reduce_max", "reduce_sum", "scalar_tensor_tensor", "tensor_tensor_scan",
        "reciprocal", "memset", "activation", "copy", "add", "mul", "select",
        "tensor_add", "tensor_sub", "tensor_mul", "tensor_max", "transpose",
        "tensor_single_scalar", "tensor_scalar_mul", "tensor_scalar_add",
    ])

    def __init__(self, eng):
        object.__setattr__(self, "_e", eng)

    def __getattr__(self, name):
        attr = getattr(self._e, name)
        if name in DrainingEngine.OPS:
            def wrapped(*a, **kw):
                self._e.drain()
                return attr(*a, **kw)
            return wrapped
        return attr


def build():
    nc = bass.Bass("TRN2", target_bir_lowering=False)
    vec = DrainingEngine(nc.vector)
    scl = DrainingEngine(nc.scalar)
    pool = DrainingEngine(nc.gpsimd)

    x_fm = nc.dram_tensor("x_fm", [D, TOK], F32, kind="ExternalInput").ap()
    w_gate = nc.dram_tensor("w_gate", [I3, D], F32, kind="ExternalInput").ap()
    w_up = nc.dram_tensor("w_up", [I3, D], F32, kind="ExternalInput").ap()
    w_down = nc.dram_tensor("w_down", [D, I3], F32, kind="ExternalInput").ap()
    W_attn = nc.dram_tensor("W_attn", [D, H3], F32, kind="ExternalInput").ap()
    b_attn = nc.dram_tensor("b_attn", [PC, 128], F32, kind="ExternalInput").ap()
    clips = nc.dram_tensor("clips", [1, 2], F32, kind="ExternalInput").ap()
    flags = nc.dram_tensor("flags", [1, 2], F32, kind="ExternalInput").ap()
    identin = nc.dram_tensor("identin", [128, 128], F32, kind="ExternalInput").ap()
    out_fm = nc.dram_tensor("out_fm", [D, TOK], F32, kind="ExternalOutput").ap()

    gq = nc.dram_tensor("gq", [I3, D], BF16).ap()
    uq = nc.dram_tensor("uq", [I3, D], BF16).ap()
    dq = nc.dram_tensor("dq", [D, I3], BF16).ap()
    Wb = nc.dram_tensor("Wbs", [D, H3], BF16).ap()
    h_scr = nc.dram_tensor("h_scr", [NT * 128, IC * T], BF16).ap()
    x1_scr = nc.dram_tensor("x1_scr", [NT * 128, TOK], BF16).ap()
    x2_scr = nc.dram_tensor("x2_scr", [NT * 128, TOK], BF16).ap()
    xn2_scr = nc.dram_tensor("xn2_scr", [NT * 128, TOK], BF16).ap()
    hl_scr = nc.dram_tensor("hl_scr", [NT * 128, TOK], BF16).ap()
    pp_scr = nc.dram_tensor("pp_scr", [NT * 128, TOK], BF16).ap()
    og_scr = nc.dram_tensor("og_scr", [NT * 128, TOK], BF16).ap()
    cmax_i = [nc.dram_tensor(f"cmax_i{i}", [1, 8], F32).ap() for i in range(4)]
    cmax_o = [nc.dram_tensor(f"cmax_o{i}", [1, 8], F32).ap() for i in range(4)]
    carry_i = nc.dram_tensor("carry_i", [8, 128], F32).ap()
    carry_o = nc.dram_tensor("carry_o", [8, 128], F32).ap()

    x3d = x_fm.rearrange("(c p) n -> p c n", p=128)
    o3d = out_fm.rearrange("(c p) n -> p c n", p=128)

    ctx = ExitStack()
    with ctx:
        s0 = Sem(ctx, nc, "s0")
        wldp = [Sem(ctx, nc, f"wld{i}") for i in range(2)]
        wst = Sem(ctx, nc, "wst")
        wgu = Sem(ctx, nc, "wgu")
        qd = Sem(ctx, nc, "qdm")
        wb = Sem(ctx, nc, "wbm")
        sx = Sem(ctx, nc, "sx")
        shl = Sem(ctx, nc, "shl")
        shstp = [Sem(ctx, nc, f"shst{i}") for i in range(2)]
        sx1p = [Sem(ctx, nc, f"sx1p{i}") for i in range(2)]
        sxn2 = Sem(ctx, nc, "sxn2")
        sout = Sem(ctx, nc, "sout")
        sxl = [Sem(ctx, nc, f"sxl{i}") for i in range(2)]
        sast = Sem(ctx, nc, "sast")
        spb = [Sem(ctx, nc, f"spb{i}") for i in range(2)]
        sx2p = [Sem(ctx, nc, f"sx2p{i}") for i in range(2)]
        qg = Sem(ctx, nc, "qg")
        sxg = [Sem(ctx, nc, f"sxg{i}") for i in range(2)]
        cc = Sem(ctx, nc, "cc")
        pe = Sem(ctx, nc, "pe")
        dve = Sem(ctx, nc, "dve")
        act = Sem(ctx, nc, "act")
        gp = Sem(ctx, nc, "gp")
        sy = Sem(ctx, nc, "sy")
        ALL_DMA_SEMS = [s0, wldp[0], wldp[1], wst, wgu, qd, wb, sx, shl, shstp[0], shstp[1],
                        sx1p[0], sx1p[1], sxn2, sout, sxl[0], sxl[1], sast,
                        spb[0], spb[1], sx2p[0], sx2p[1], qg, sxg[0], sxg[1]]

        def dma(q, out, in_, sem, **kw):
            q.dma_start(out=out, in_=in_, **kw).then_inc(sem.h, 16)
            sem.v += 16
            return sem.v

        def tick(instr, sem):
            sem.v += 1
            instr.then_inc(sem.h, 1)
            return sem.v

        ENGS = [nc.tensor, nc.vector, nc.scalar, nc.gpsimd, nc.sync]

        def fence_snapshot():
            return [(s.h, s.v) for s in [pe, dve, act, gp] + ALL_DMA_SEMS]

        def queue_fence(snap):
            for eng in (nc.sync, nc.scalar):
                for h, v in snap:
                    if v > 0:
                        eng.wait_ge(h, v)

        def barrier():
            snap = [(s.h, s.v) for s in [pe, dve, act, gp, cc] + ALL_DMA_SEMS]
            for eng in ENGS:
                for h, v in snap:
                    if v > 0:
                        eng.wait_ge(h, v)

        ones128 = ctx.enter_context(nc.sbuf_tensor("ones128", [128, 128], BF16))
        onesrow = ctx.enter_context(nc.sbuf_tensor("onesrow", [1, 128], F32))
        identf = ctx.enter_context(nc.sbuf_tensor("identf", [128, 128], F32))
        zerosT = ctx.enter_context(nc.sbuf_tensor("zerosT", [128, T], BF16))
        bias_t = ctx.enter_context(nc.sbuf_tensor("bias_t", [128, PC], F32))
        negb_t = ctx.enter_context(nc.sbuf_tensor("negb_t", [128, PC], F32))
        sclips = ctx.enter_context(nc.sbuf_tensor("sclips", [1, 2], F32))
        sg_t = ctx.enter_context(nc.sbuf_tensor("sg_t", [128, IC], F32))
        su_t = ctx.enter_context(nc.sbuf_tensor("su_t", [128, IC], F32))
        sd_t = ctx.enter_context(nc.sbuf_tensor("sd_t", [128, DC], F32))
        sgc = [ctx.enter_context(nc.sbuf_tensor(f"sgc{i}", [128, IC], F32)) for i in range(2)]
        suc = [ctx.enter_context(nc.sbuf_tensor(f"suc{i}", [128, IC], F32)) for i in range(2)]
        sdc = [ctx.enter_context(nc.sbuf_tensor(f"sdc{i}", [128, DC], F32)) for i in range(2)]
        sv = [ctx.enter_context(nc.sbuf_tensor(f"sv{i}", [128, 2], F32)) for i in range(4)]
        flv = ctx.enter_context(nc.sbuf_tensor("flv", [128, 2], F32))
        state_h = ctx.enter_context(nc.sbuf_tensor("state_h", [128, DC], F32))
        state_p = ctx.enter_context(nc.sbuf_tensor("state_p", [128, DC], F32))
        h_in_t = ctx.enter_context(nc.sbuf_tensor("h_in_t", [128, DC], F32))
        amax8 = [ctx.enter_context(nc.sbuf_tensor(f"amax8_{i}", [128, NT * IC], F32)) for i in range(4)]
        cmax_sb = ctx.enter_context(nc.sbuf_tensor("cmax_sb", [1, 8], F32))
        gmax_sb = ctx.enter_context(nc.sbuf_tensor("gmax_sb", [1, 8], F32))
        srow = ctx.enter_context(nc.sbuf_tensor("srow", [1, 2], F32))
        carr_sb = ctx.enter_context(nc.sbuf_tensor("carr_sb", [8, 128], F32))
        lm_g = ctx.enter_context(nc.sbuf_tensor("lm_g", [128, 1], F32))
        biasC = ctx.enter_context(nc.sbuf_tensor("biasC", [128, 1], F32))
        negC = ctx.enter_context(nc.sbuf_tensor("negC", [128, 1], F32))
        XN = {}

        def xs(t, c):
            off = t * (DC * T) + c * T
            return XN["t"][:, off:off + T]

        def xnsl(t):
            return XN["t"][:, t * (DC * T):(t + 1) * (DC * T)]

        v_ident = dma(nc.sync, identf[:], identin, s0)
        v_clip = dma(nc.sync, sclips[:], clips, s0)
        vec.memset(ones128[:], 1.0)
        vec.memset(onesrow[:], 1.0)
        vec.memset(zerosT[:], 0.0)
        vec.memset(state_h[:], 0.0)
        vec.memset(state_p[:], 1.0)
        vec.memset(cmax_sb[:], 0.0)
        v_memset = tick(vec.memset(srow[:], 0.0), dve)

        with nc.sbuf_tensor("btmp", [PC, 128], F32) as btmp, \
             nc.sbuf_tensor("frow", [1, 2], F32) as frow, \
             nc.psum_tensor("pb", [128, PC], F32) as pb, \
             nc.psum_tensor("pf", [128, 2], F32) as pf:
            v_b = dma(nc.sync, btmp[:], b_attn, s0)
            v_f = dma(nc.sync, frow[:], flags, s0)
            scl.wait_ge(s0.h, s0.v)
            v_sig = tick(scl.activation(out=sclips[:], in_=sclips[:], func=AF.Sigmoid), act)
            nc.tensor.wait_ge(s0.h, s0.v)
            nc.tensor.wait_ge(dve.h, v_memset)
            nc.tensor.transpose(pb[:], btmp[:], identf[0:PC, 0:PC])
            v_pe0 = tick(nc.tensor.matmul(pf[:], onesrow[:], frow[:], start=True, stop=True), pe)
            vec.wait_ge(pe.h, v_pe0)
            vec.tensor_copy(out=bias_t[:], in_=pb[:])
            vec.tensor_scalar(out=negb_t[:], in0=pb[:], scalar1=-1.0, scalar2=None, op0=ALU.mult)
            v_flv = tick(vec.tensor_copy(out=flv[:], in_=pf[:]), dve)

        def ln_front(xb_tile, sq_tile, pstat):
            v_sq = tick(vec.tensor_tensor(out=sq_tile[:], in0=xb_tile[:],
                                          in1=xb_tile[:], op=ALU.mult), dve)
            nc.tensor.wait_ge(dve.h, v_sq)
            for c in range(DC):
                nc.tensor.matmul(pstat[:, 0:T], ones128[:], xb_tile[:, c * T:(c + 1) * T],
                                 start=(c == 0), stop=(c == DC - 1))
            for c in range(DC):
                mms = nc.tensor.matmul(pstat[:, T:2 * T], ones128[:], sq_tile[:, c * T:(c + 1) * T],
                                       start=(c == 0), stop=(c == DC - 1))
            return tick(mms, pe)

        def ln_back(xb_tile, pstat, mrows, xn_out, v_st):
            m_t, var_t, rstd_t, mb, rstdb = (mrows["m"], mrows["var"], mrows["rstd"],
                                             mrows["mb"], mrows["rstdb"])
            vec.wait_ge(pe.h, v_st)
            vec.tensor_scalar(out=m_t[:], in0=pstat[:, 0:T], scalar1=1.0 / D,
                              scalar2=None, op0=ALU.mult)
            vec.tensor_scalar(out=var_t[:], in0=pstat[:, T:2 * T], scalar1=1.0 / D,
                              scalar2=EPS, op0=ALU.mult, op1=ALU.add)
            vec.tensor_tensor(out=rstd_t[:], in0=m_t[:], in1=m_t[:], op=ALU.mult)
            v_var = tick(vec.tensor_tensor(out=var_t[:], in0=var_t[:], in1=rstd_t[:],
                                           op=ALU.subtract), dve)
            scl.wait_ge(dve.h, v_var)
            scl.activation(out=rstd_t[:], in_=var_t[:], func=AF.Ln)
            v_rs = tick(scl.activation(out=rstd_t[:], in_=rstd_t[:], func=AF.Exp,
                                       scale=-0.5), act)
            vec.wait_ge(act.h, v_rs)
            vec.tensor_copy(out=mb[:], in_=m_t[:])
            vec.tensor_copy(out=rstdb[:], in_=rstd_t[:])
            xo3 = xn_out.rearrange("p (c n) -> p c n", n=T)
            mbb = mb[:].unsqueeze(1).broadcast_to([128, DC, T])
            rbb = rstdb[:].unsqueeze(1).broadcast_to([128, DC, T])
            vec.tensor_tensor(out=xo3, in0=xb_tile[:].rearrange("p (c n) -> p c n", n=T),
                              in1=mbb, op=ALU.subtract)
            return tick(vec.tensor_tensor(out=xo3, in0=xo3, in1=rbb, op=ALU.mult), dve)

        def global_max_barrier(idx, amax_tile, n_cols):
            with nc.psum_tensor(f"pt{idx}", [1, 128], F32) as pt, \
                 nc.psum_tensor(f"pv{idx}", [128, 2], F32) as pv:
                lm = lm_g
                v0 = tick(vec.tensor_reduce(out=lm[:], in_=amax_tile[:, 0:n_cols],
                                                  axis=AX.X, op=ALU.max), dve)
                nc.tensor.wait_ge(dve.h, v0)
                nc.tensor.wait_ge(act.h, act.v)
                v1 = tick(nc.tensor.transpose(pt[:], lm[:], identf[:]), pe)
                vec.wait_ge(pe.h, v1)
                v2 = tick(vec.tensor_reduce(out=cmax_sb[0:1, 0:1], in_=pt[:],
                                                  axis=AX.X, op=ALU.max), dve)
                nc.gpsimd.wait_ge(dve.h, v2)
                v3 = dma(nc.gpsimd, cmax_i[idx], cmax_sb[:], qg)
                nc.gpsimd.wait_ge(qg.h, v3)
                cinst = nc.gpsimd.collective_compute(
                    "AllReduce", ALU.max, replica_groups=[list(range(8))],
                    ins=[cmax_i[idx]], outs=[cmax_o[idx]])
                v4 = tick(cinst, cc)
                nc.gpsimd.wait_ge(cc.h, v4)
                v5 = dma(nc.gpsimd, gmax_sb[:], cmax_o[idx], qg)
                vec.wait_ge(qg.h, v5)
                clip_col = 0 if idx in (0, 2) else 1
                vec.tensor_tensor(out=srow[0:1, 0:1], in0=gmax_sb[0:1, 0:1],
                                        in1=sclips[0:1, clip_col:clip_col + 1], op=ALU.mult)
                vec.tensor_scalar(out=srow[0:1, 0:1], in0=srow[0:1, 0:1],
                                        scalar1=1.0 / QMAX, scalar2=None, op0=ALU.mult)
                v6 = tick(vec.reciprocal(out=srow[0:1, 1:2], in_=srow[0:1, 0:1]), dve)
                nc.tensor.wait_ge(dve.h, v6)
                v7 = tick(nc.tensor.matmul(pv[:], onesrow[:], srow[:], start=True, stop=True), pe)
                vec.wait_ge(pe.h, v7)
                v8 = tick(vec.tensor_copy(out=sv[idx][:], in_=pv[:]), dve)
            return v8

        def quant_resident(svec, n_blocks, tscr):
            for blk in range(n_blocks):
                sl = xnsl(blk)
                vec.tensor_scalar(out=tscr[:, 0:TOK], in0=sl, scalar1=svec[:, 1:2],
                                        scalar2=MAGIC, op0=ALU.mult, op1=ALU.add)
                vec.tensor_scalar(out=tscr[:, 0:TOK], in0=tscr[:, 0:TOK],
                                        scalar1=MAGIC - 128.0, scalar2=MAGIC + 127.0,
                                        op0=ALU.max, op1=ALU.min)
                tick(vec.tensor_scalar(out=sl, in0=tscr[:, 0:TOK], scalar1=MAGIC,
                                             scalar2=None, op0=ALU.subtract), dve)

        seg1 = ExitStack()
        XN["t"] = seg1.enter_context(nc.sbuf_tensor("xn_seg1", [128, DC * TOK], BF16))
        with ExitStack() as pctx:
            sb = lambda n, s, d: pctx.enter_context(nc.sbuf_tensor(n, s, d))
            WBW = 11264
            wraw = [sb(f"wraw{j}", [128, WBW], BF16) for j in range(2)]
            qout = sb("qout", [128, WBW], BF16)
            tq = [sb("tq0", [128, 2816], F32)] * 2
            rmax = sb("rmax", [128, 1], F32)
            rsvt2 = [sb(f"rsvt{j}", [128, 12], F32) for j in range(2)]
            x0b = [sb(f"x0b{j}", [128, TOK], BF16) for j in range(2)]
            sqb = [sb(f"sqb{j}", [128, TOK], BF16) for j in range(2)]
            mrows2 = []
            for j in range(2):
                mrows2.append(dict(
                    m=sb(f"m_t{j}", [128, T], F32), var=sb(f"var_t{j}", [128, T], F32),
                    rstd=sb(f"rstd_t{j}", [128, T], F32), mb=sb(f"mb{j}", [128, T], BF16),
                    rstdb=sb(f"rstdb{j}", [128, T], BF16), t1=sb(f"t1b{j}", [128, T], BF16)))
            pstat1 = [pctx.enter_context(nc.psum_tensor(f"pstat1_{j}", [128, 2 * T], F32))
                      for j in range(2)]

            vec.memset(biasC[:], MAGIC)
            v_bc = tick(vec.memset(negC[:], -MAGIC), dve)
            nc.scalar.wait_ge(dve.h, v_bc)
            nc.gpsimd.wait_ge(dve.h, v_bc)

            batches = []
            for (w_in, scr, n_ch, width, sc_t) in [
                    (w_gate, gq, IC, D, sg_t), (w_up, uq, IC, D, su_t),
                    (w_down, dq, DC, I3, sd_t)]:
                w3d = w_in.rearrange("(c p) n -> p c n", p=128)
                s3d = scr.rearrange("(c p) n -> p c n", p=128)
                nh = WBW // width
                for c0 in range(0, n_ch, nh):
                    batches.append((w3d, s3d, c0, min(c0 + nh, n_ch), width, sc_t))
            NB = len(batches)

            vload = {}
            act_done = {}
            q_done = {}
            rs_done = {}
            cast_done = {}
            ln1_pe = {}

            def issue_load(i):
                w3d, _, c0, c1, width, _ = batches[i]
                if i >= 2:
                    if rs_done.get(i - 2) is not None:
                        nc.gpsimd.wait_ge(dve.h, rs_done[i - 2])
                    nc.gpsimd.wait_ge(act.h, act_done[i - 2])
                dst = wraw[i % 2][:, 0:(c1 - c0) * width].rearrange("p (c n) -> p c n", n=width)
                vload[i] = dma(nc.gpsimd, dst, w3d[:, c0:c1, :], wldp[i % 2])

            def emit_quant_batch(i):
                w3d, s3d, c0, c1, width, sc_t = batches[i]
                nb = c1 - c0
                wr = wraw[i % 2]
                rsvt = rsvt2[i % 2]
                if sc_t is not None:
                    vec.wait_ge(wldp[i % 2].h, vload[i])
                    if i >= 2 and rs_done.get(i - 2) is not None:
                        vec.wait_ge(act.h, act_done[i - 2])
                    last_rs = None
                    for c in range(nb):
                        wsl = wr[:, c * width:(c + 1) * width]
                        vec.tensor_reduce(out=rmax[:], in_=wsl, axis=AX.X, op=ALU.max,
                                          apply_absolute_value=True)
                        vec.tensor_scalar(out=sc_t[:, c0 + c:c0 + c + 1], in0=rmax[:],
                                          scalar1=1.0 / QMAX, scalar2=None, op0=ALU.mult)
                        last_rs = tick(vec.reciprocal(out=rsvt[:, c:c + 1],
                                                      in_=sc_t[:, c0 + c:c0 + c + 1]), dve)
                    rs_done[i] = last_rs
                    scl.wait_ge(dve.h, rs_done[i])
                    scl.wait_ge(wst.h, wst.v)
                    last_q = None
                    for c in range(nb):
                        wsl = wr[:, c * width:(c + 1) * width]
                        scl.activation(out=tq[c % 2][:, 0:width], in_=wsl,
                                       func=AF.Identity,
                                       scale=rsvt[:, c:c + 1], bias=biasC[:])
                        last_q = tick(scl.activation(out=qout[:, c * width:(c + 1) * width],
                                                     in_=tq[c % 2][:, 0:width], func=AF.Identity,
                                                     bias=negC[:]), act)
                    act_done[i] = last_q
                else:
                    rs_done[i] = None
                    scl.wait_ge(wldp[i % 2].h, vload[i])
                    scl.wait_ge(wst.h, wst.v)
                    last_q = None
                    for c in range(nb):
                        last_q = tick(scl.copy(out=qout[:, c * width:(c + 1) * width],
                                               in_=wr[:, c * width:(c + 1) * width]), act)
                        act_done[i] = last_q
                q_done[i] = last_q
                nc.scalar.wait_ge(act.h, last_q)
                dma(nc.scalar, s3d[:, c0:c1, :],
                    qout[:, 0:nb * width].rearrange("p (c n) -> p c n", n=width), wst)

            ln1_apply = {}
            xload1 = {}

            def issue_xload(t):
                j = t % 2
                if t >= 2:
                    nc.gpsimd.wait_ge(pe.h, ln1_pe[t - 2])
                    nc.gpsimd.wait_ge(dve.h, ln1_apply[t - 2])
                xload1[t] = dma(nc.gpsimd, x0b[j][:].rearrange("p (c n) -> p c n", n=T),
                                x3d[:, :, t * T:(t + 1) * T], sxg[j])

            def ln1_front(t):
                j = t % 2
                v_x = xload1[t]
                vec.wait_ge(sxg[j].h, v_x)
                ln1_pe[t] = ln_front(x0b[j], sqb[j], pstat1[j])

            def ln1_back(t):
                j = t % 2
                ln1_apply[t] = ln_back(x0b[j], pstat1[j], mrows2[j], xnsl(t), ln1_pe[t])
                tick(vec.tensor_reduce(out=amax8[0][:, t:t + 1], in_=xnsl(t),
                                       axis=AX.X, op=ALU.max,
                                       apply_absolute_value=True), dve)
                if t + 2 < NT:
                    issue_xload(t + 2)

            def emit_ln1_tile(t):
                ln1_front(t)
                if t > 0:
                    ln1_back(t - 1)
                if t == NT - 1:
                    ln1_back(t)

            issue_xload(0)
            issue_xload(1)
            issue_load(0)
            issue_load(1)
            nt_done = 0
            for i in range(NB):
                emit_quant_batch(i)
                if i + 2 < NB:
                    issue_load(i + 2)
                if nt_done < NT:
                    emit_ln1_tile(nt_done)
                    nt_done += 1
                if nt_done < NT and i >= NB - 2:
                    emit_ln1_tile(nt_done)
                    nt_done += 1
            while nt_done < NT:
                emit_ln1_tile(nt_done)
                nt_done += 1
            dma(nc.gpsimd, Wb, W_attn, qg)

        def gu_phase(call, wgu_sb, v_wready, amc, svec, v_scales):
            with ExitStack() as gctx:
                stag = gctx.enter_context(nc.sbuf_tensor(f"stag{call}", [128, IC * T], BF16))
                tscrq = gctx.enter_context(nc.sbuf_tensor(f"tscrq{call}", [128, 1024], F32))
                qx_ready = {}
                quant_prev = [None]

                def quant_quarter_x(tt, qq):
                    sl = XN["t"][:, tt * (DC * T) + qq * 1024:tt * (DC * T) + (qq + 1) * 1024]
                    if quant_prev[0] is not None:
                        scl.wait_ge(dve.h, quant_prev[0])
                    va = tick(scl.activation(out=tscrq[:], in_=sl, func=AF.Identity,
                                             scale=svec[:, 1:2], bias=biasC[:]), act)
                    vec.wait_ge(act.h, va)
                    vec.tensor_scalar(out=tscrq[:], in0=tscrq[:], scalar1=MAGIC - 128.0,
                                      scalar2=MAGIC + 127.0, op0=ALU.max, op1=ALU.min)
                    v = tick(vec.tensor_scalar(out=sl, in0=tscrq[:], scalar1=MAGIC,
                                               scalar2=None, op0=ALU.subtract), dve)
                    quant_prev[0] = v
                    if qq == 3:
                        qx_ready[tt] = v
                NR = 4
                sgl = [gctx.enter_context(nc.sbuf_tensor(f"sgl{call}_{j}", [128, T], BF16)) for j in range(NR)]
                utl = [gctx.enter_context(nc.sbuf_tensor(f"utl{call}_{j}", [128, T], BF16)) for j in range(NR)]
                pg = [gctx.enter_context(nc.psum_tensor(f"pg{call}_{j}", [128, T], F32)) for j in range(NR)]
                pu = [gctx.enter_context(nc.psum_tensor(f"pu{call}_{j}", [128, T], F32)) for j in range(NR)]
                nc.tensor.wait_ge(wgu.h, v_wready)
                scl.wait_ge(dve.h, v_scales)
                for qq in range(4):
                    quant_quarter_x(0, qq)
                silu_done = {}
                h_done = {}
                st_half = {}
                HSPL = 11
                it = 0
                for t in range(NT):
                    nc.tensor.wait_ge(dve.h, qx_ready[t])
                    for c in range(IC):
                        if t + 1 < NT and c in (3, 7, 11, 15):
                            quant_quarter_x(t + 1, (c - 3) // 4)
                        j = it % NR
                        if it >= NR:
                            nc.tensor.wait_ge(act.h, silu_done[it - NR])
                            nc.tensor.wait_ge(dve.h, h_done[it - NR])
                        for k in range(DC):
                            mmg = nc.tensor.matmul(pg[j][:], wgu_sb[:, k * I3 + 128 * c:k * I3 + 128 * (c + 1)],
                                                   XN["t"][:, t * (DC * T) + k * T:t * (DC * T) + (k + 1) * T],
                                                   start=(k == 0), stop=(k == DC - 1))
                        vg = tick(mmg, pe)
                        for k in range(DC):
                            mmu = nc.tensor.matmul(pu[j][:], wgu_sb[:, (8 + k) * I3 + 128 * c:(8 + k) * I3 + 128 * (c + 1)],
                                                   XN["t"][:, t * (DC * T) + k * T:t * (DC * T) + (k + 1) * T],
                                                   start=(k == 0), stop=(k == DC - 1))
                        vu = tick(mmu, pe)
                        scl.wait_ge(pe.h, vg)
                        if it >= NR:
                            scl.wait_ge(dve.h, h_done[it - NR])
                        silu_done[it] = tick(scl.activation(
                            out=sgl[j][:], in_=pg[j][:], func=AF.Silu,
                            scale=sgc[call][:, c:c + 1]), act)
                        vec.wait_ge(pe.h, vu)
                        vec.tensor_scalar(out=utl[j][:], in0=pu[j][:],
                                                scalar1=suc[call][:, c:c + 1], scalar2=None,
                                                op0=ALU.mult)
                        vec.wait_ge(act.h, silu_done[it])
                        if t > 0 and c in (0, HSPL):
                            half = 0 if c == 0 else 1
                            vec.wait_ge(shstp[half].h, st_half[(t - 1, half)])
                        h_done[it] = tick(vec.tensor_tensor(
                            out=stag[:, c * T:(c + 1) * T], in0=sgl[j][:], in1=utl[j][:],
                            op=ALU.mult), dve)
                        tick(vec.tensor_reduce(out=amc[:, t * IC + c:t * IC + c + 1],
                                               in_=stag[:, c * T:(c + 1) * T],
                                               axis=AX.X, op=ALU.max,
                                               apply_absolute_value=True), dve)
                        if c == HSPL - 1:
                            nc.sync.wait_ge(dve.h, h_done[it])
                            st_half[(t, 0)] = dma(nc.sync, h_scr[128 * t:128 * (t + 1), 0:HSPL * T],
                                                  stag[:, 0:HSPL * T], shstp[0])
                        it += 1
                    nc.sync.wait_ge(dve.h, h_done[it - 1])
                    st_half[(t, 1)] = dma(nc.sync, h_scr[128 * t:128 * (t + 1), HSPL * T:],
                                          stag[:, HSPL * T:], shstp[1])
                nc.sync.wait_ge(shstp[0].h, st_half[(NT - 1, 0)])
                nc.sync.wait_ge(shstp[1].h, st_half[(NT - 1, 1)])
            return

        def load_wgu(call, octx):
            wgu_sb = octx.enter_context(nc.sbuf_tensor(f"wgu{call}", [128, 16 * I3], BF16))
            v = 0
            for k in range(DC):
                v = dma(nc.scalar, wgu_sb[:, k * I3:(k + 1) * I3],
                        gq[:, 128 * k:128 * (k + 1)], wgu, transpose=True)
            for k in range(DC):
                v = dma(nc.scalar, wgu_sb[:, (8 + k) * I3:(9 + k) * I3],
                        uq[:, 128 * k:128 * (k + 1)], wgu, transpose=True)
            return wgu_sb, v

        def down_phase(call):
            first = (call == 0)
            QF = IC * T // 4
            with ExitStack() as dctx:
                qdT = dctx.enter_context(nc.sbuf_tensor(f"qdT{call}", [128, IC * D], BF16))
                hbuf = dctx.enter_context(nc.sbuf_tensor(f"hbuf{call}", [128, IC * T], BF16))
                qh = [dctx.enter_context(nc.sbuf_tensor(f"qh{call}_{j}", [128, IC * T], BF16)) for j in range(2)]
                tscr = dctx.enter_context(nc.sbuf_tensor(f"tscr{call}", [128, QF], F32))
                xres = dctx.enter_context(nc.sbuf_tensor(f"xres{call}", [128, TOK], F32 if first else BF16))
                nxout = 2 if first else 1
                xout = [dctx.enter_context(nc.sbuf_tensor(f"xout{call}_{j}", [128, TOK],
                                                          BF16 if first else F32)) for j in range(nxout)]
                pd = [dctx.enter_context(nc.psum_tensor(f"pd{call}_{j}", [128, T], F32)) for j in range(4)]
                if first:
                    sqb = [dctx.enter_context(nc.sbuf_tensor(f"sqb2_{j}", [128, TOK], BF16)) for j in range(2)]
                    m_t = dctx.enter_context(nc.sbuf_tensor("m_t2", [128, T], F32))
                    var_t = dctx.enter_context(nc.sbuf_tensor("var_t2", [128, T], F32))
                    rstd_t = dctx.enter_context(nc.sbuf_tensor("rstd_t2", [128, T], F32))
                    mb = dctx.enter_context(nc.sbuf_tensor("mb2", [128, T], BF16))
                    rstdb = dctx.enter_context(nc.sbuf_tensor("rstdb2", [128, T], BF16))
                    t1b = dctx.enter_context(nc.sbuf_tensor("t1b2", [128, T], BF16))
                    xn2st = dctx.enter_context(nc.sbuf_tensor("xn2st", [128, TOK], BF16))
                    pstat = [dctx.enter_context(nc.psum_tensor(f"pstat2_{j}", [128, 2 * T], F32)) for j in range(2)]
                v_wd = 0
                for k in range(IC):
                    v_wd = dma(nc.scalar, qdT[:, k * D:(k + 1) * D],
                               dq[:, 128 * k:128 * (k + 1)], qd, transpose=True)
                svec = sv[1] if first else sv[3]
                scomb = sdc[call]
                qh_ready = {}
                mm_done = {}
                stt_done = {}
                hload = {}
                xload = {}
                out_store = {}
                xn2_store = {}
                ln_pe = {}
                v_sq = {}
                evac_dve = {}
                nc.tensor.wait_ge(qd.h, v_wd)
                hload[0] = dma(nc.sync, hbuf[:], h_scr[0:128, :], shl)
                if first:
                    xload[0] = dma(nc.sync, xres[:].rearrange("p (c n) -> p c n", n=T),
                                   x3d[:, :, 0:T], sx)
                else:
                    xload[0] = dma(nc.sync, xres[:], x2_scr[0:128, :], sx)

                qprev = [None]

                def quant_quarter(tt, q):
                    sl_in = hbuf[:, q * QF:(q + 1) * QF]
                    sl_out = qh[tt % 2][:, q * QF:(q + 1) * QF]
                    if qprev[0] is not None:
                        scl.wait_ge(dve.h, qprev[0])
                    va = tick(scl.activation(out=tscr[:], in_=sl_in, func=AF.Identity,
                                             scale=svec[:, 1:2], bias=biasC[:]), act)
                    vec.wait_ge(act.h, va)
                    vec.tensor_scalar(out=tscr[:], in0=tscr[:], scalar1=MAGIC - 128.0,
                                      scalar2=MAGIC + 127.0, op0=ALU.max, op1=ALU.min)
                    v = tick(vec.tensor_scalar(out=sl_out, in0=tscr[:], scalar1=MAGIC,
                                               scalar2=None, op0=ALU.subtract), dve)
                    qprev[0] = v
                    if q == 3:
                        qh_ready[tt] = v

                def ln2_rest(tt):
                    vec.wait_ge(pe.h, ln_pe[tt])
                    ps = pstat[tt % 2]
                    vec.tensor_scalar(out=m_t[:], in0=ps[:, 0:T], scalar1=1.0 / D,
                                      scalar2=None, op0=ALU.mult)
                    vec.tensor_scalar(out=var_t[:], in0=ps[:, T:2 * T], scalar1=1.0 / D,
                                      scalar2=EPS, op0=ALU.mult, op1=ALU.add)
                    vec.tensor_tensor(out=rstd_t[:], in0=m_t[:], in1=m_t[:], op=ALU.mult)
                    v_var = tick(vec.tensor_tensor(out=var_t[:], in0=var_t[:], in1=rstd_t[:],
                                                   op=ALU.subtract), dve)
                    scl.wait_ge(dve.h, v_var)
                    scl.activation(out=rstd_t[:], in_=var_t[:], func=AF.Ln)
                    v_r = tick(scl.activation(out=rstd_t[:], in_=rstd_t[:], func=AF.Exp,
                                              scale=-0.5), act)
                    vec.wait_ge(act.h, v_r)
                    if tt > 0:
                        vec.wait_ge(sxn2.h, xn2_store[tt - 1])
                    vec.tensor_copy(out=mb[:], in_=m_t[:])
                    vec.tensor_copy(out=rstdb[:], in_=rstd_t[:])
                    xo3 = xn2st[:].rearrange("p (c n) -> p c n", n=T)
                    mbb = mb[:].unsqueeze(1).broadcast_to([128, DC, T])
                    rbb = rstdb[:].unsqueeze(1).broadcast_to([128, DC, T])
                    vec.tensor_tensor(out=xo3, in0=xout[tt % 2][:].rearrange("p (c n) -> p c n", n=T),
                                      in1=mbb, op=ALU.subtract)
                    last = tick(vec.tensor_tensor(out=xo3, in0=xo3, in1=rbb, op=ALU.mult), dve)
                    evac_dve[tt] = last
                    nc.scalar.wait_ge(dve.h, last)
                    xn2_store[tt] = dma(nc.scalar, xn2_scr[128 * tt:128 * (tt + 1), :],
                                        xn2st[:], sxn2)

                vec.wait_ge(shl.h, hload[0])
                scl.wait_ge(shl.h, hload[0])
                scl.wait_ge(dve.h, dve.v)
                for q in range(4):
                    quant_quarter(0, q)
                nc.sync.wait_ge(dve.h, qh_ready[0])
                if NT > 1:
                    hload[1] = dma(nc.sync, hbuf[:], h_scr[128:256, :], shl)

                for t in range(NT):
                    xo = xout[t % nxout]
                    nc.tensor.wait_ge(dve.h, qh_ready[t])
                    for m in range(DC):
                        j = m % 4
                        if m >= 4:
                            nc.tensor.wait_ge(dve.h, stt_done[(t, m - 4)])
                        elif t > 0:
                            nc.tensor.wait_ge(dve.h, stt_done[(t - 1, m + 4)])
                        for k in range(IC):
                            mmd = nc.tensor.matmul(pd[j][:], qdT[:, k * D + 128 * m:k * D + 128 * (m + 1)],
                                                   qh[t % 2][:, k * T:(k + 1) * T],
                                                   start=(k == 0), stop=(k == IC - 1))
                        mm_done[(t, m)] = tick(mmd, pe)
                        if first and m == 1 and t > 0:
                            tp = t - 1
                            nc.tensor.wait_ge(dve.h, v_sq[tp])
                            if tp >= 2:
                                nc.tensor.wait_ge(dve.h, evac_dve[tp - 2])
                            ps = pstat[tp % 2]
                            for c in range(DC):
                                nc.tensor.matmul(ps[:, 0:T], ones128[:],
                                                 xout[tp % 2][:, c * T:(c + 1) * T],
                                                 start=(c == 0), stop=(c == DC - 1))
                            for c in range(DC):
                                mms = nc.tensor.matmul(ps[:, T:2 * T], ones128[:],
                                                       sqb[tp % 2][:, c * T:(c + 1) * T],
                                                       start=(c == 0), stop=(c == DC - 1))
                            ln_pe[tp] = tick(mms, pe)
                        vec.wait_ge(pe.h, mm_done[(t, m)])
                        if m == 0:
                            vec.wait_ge(sx.h, xload[t])
                            if t >= nxout:
                                if first:
                                    vec.wait_ge(sx1p[t % 2].h, out_store[t - nxout])
                                else:
                                    vec.wait_ge(sout.h, out_store[t - nxout])
                        stt_done[(t, m)] = tick(vec.scalar_tensor_tensor(
                            out=xo[:, m * T:(m + 1) * T], in0=pd[j][:],
                            scalar=scomb[:, m:m + 1], in1=xres[:, m * T:(m + 1) * T],
                            op0=ALU.mult, op1=ALU.add), dve)
                        if m < 4 and t + 1 < NT:
                            q = m
                            if q == 0:
                                scl.wait_ge(shl.h, hload[t + 1])
                                vec.wait_ge(shl.h, hload[t + 1])
                                if t >= 1:
                                    vec.wait_ge(pe.h, mm_done[(t - 1, DC - 1)])
                            quant_quarter(t + 1, q)
                    if t + 1 < NT:
                        nc.sync.wait_ge(dve.h, stt_done[(t, DC - 1)])
                        if first:
                            xload[t + 1] = dma(nc.sync, xres[:].rearrange("p (c n) -> p c n", n=T),
                                               x3d[:, :, (t + 1) * T:(t + 2) * T], sx)
                        else:
                            xload[t + 1] = dma(nc.sync, xres[:],
                                               x2_scr[128 * (t + 1):128 * (t + 2), :], sx)
                    if t + 2 < NT:
                        nc.sync.wait_ge(dve.h, qh_ready[t + 1])
                        hload[t + 2] = dma(nc.sync, hbuf[:], h_scr[128 * (t + 2):128 * (t + 3), :], shl)
                    nc.scalar.wait_ge(dve.h, stt_done[(t, DC - 1)])
                    if first:
                        out_store[t] = dma(nc.scalar, x1_scr[128 * t:128 * (t + 1), :], xo[:], sx1p[t % 2])
                        v_sq[t] = tick(vec.tensor_tensor(out=sqb[t % 2][:], in0=xo[:],
                                                         in1=xo[:], op=ALU.mult), dve)
                        if t > 0:
                            ln2_rest(t - 1)
                    else:
                        out_store[t] = dma(nc.scalar, o3d[:, :, t * T:(t + 1) * T],
                                           xo[:].rearrange("p (c n) -> p c n", n=T), sout)
                snap_end = fence_snapshot()
                if first:
                    tp = NT - 1
                    nc.tensor.wait_ge(dve.h, v_sq[tp])
                    nc.tensor.wait_ge(dve.h, evac_dve[tp - 2])
                    ps = pstat[tp % 2]
                    for c in range(DC):
                        nc.tensor.matmul(ps[:, 0:T], ones128[:],
                                         xout[tp % 2][:, c * T:(c + 1) * T],
                                         start=(c == 0), stop=(c == DC - 1))
                    for c in range(DC):
                        mms = nc.tensor.matmul(ps[:, T:2 * T], ones128[:],
                                               sqb[tp % 2][:, c * T:(c + 1) * T],
                                               start=(c == 0), stop=(c == DC - 1))
                    ln_pe[tp] = tick(mms, pe)
                    ln2_rest(tp)
            return snap_end, xn2_store

        barrier()
        with ExitStack() as octx0:
            wgu_sb0, v_wr0 = load_wgu(0, octx0)
            global_max_barrier(0, amax8[0], NT)
            vec.wait_ge(act.h, v_sig)
            vec.tensor_scalar(out=sgc[0][:], in0=sg_t[:], scalar1=sv[0][:, 0:1],
                              scalar2=None, op0=ALU.mult)
            v_sc0 = tick(vec.tensor_scalar(out=suc[0][:], in0=su_t[:], scalar1=sv[0][:, 0:1],
                                           scalar2=None, op0=ALU.mult), dve)
            gu_phase(0, wgu_sb0, v_wr0, amax8[1], sv[0], v_sc0)
        seg1.close()
        pre1 = fence_snapshot()
        global_max_barrier(1, amax8[1], NT * IC)
        tick(vec.tensor_scalar(out=sdc[0][:], in0=sd_t[:], scalar1=sv[1][:, 0:1],
                                     scalar2=0.5, op0=ALU.mult, op1=ALU.mult), dve)
        queue_fence(pre1)
        snap_d0, xn2_store_d0 = down_phase(0)
        queue_fence(snap_d0)
        for s_ in (pe, dve, act, cc):
            vec.wait_ge(s_.h, s_.v)
            scl.wait_ge(s_.h, s_.v)
            nc.tensor.wait_ge(s_.h, s_.v)
            nc.gpsimd.wait_ge(s_.h, s_.v)

        with ExitStack() as actx:
            Wbt = actx.enter_context(nc.sbuf_tensor("Wbt", [128, DC * H3], BF16))
            xn2b = [actx.enter_context(nc.sbuf_tensor(f"xn2b{j}", [128, TOK], BF16)) for j in range(2)]
            ig_t = actx.enter_context(nc.sbuf_tensor("ig_t", [128, TOK], BF16))
            f_t = actx.enter_context(nc.sbuf_tensor("f_t", [128, TOK], BF16))
            th_t = actx.enter_context(nc.sbuf_tensor("th_t", [128, TOK], BF16))
            u_t = actx.enter_context(nc.sbuf_tensor("u_t", [128, TOK], BF16))
            og_st = actx.enter_context(nc.sbuf_tensor("og_st", [128, TOK], BF16))
            hl_st = actx.enter_context(nc.sbuf_tensor("hl_st", [128, TOK], BF16))
            pp_st = actx.enter_context(nc.sbuf_tensor("pp_st", [128, TOK], BF16))
            pa = [actx.enter_context(nc.psum_tensor(f"pa{j}", [128, T], F32)) for j in range(4)]
            pcar = actx.enter_context(nc.psum_tensor("pcar", [8, 128], F32))
            pcar2 = actx.enter_context(nc.psum_tensor("pcar2", [128, 8], F32))
            v_wb = 0
            for k in range(DC):
                v_wb = dma(nc.scalar, Wbt[:, k * H3:(k + 1) * H3],
                           Wb[128 * k:128 * (k + 1), :], wb)
            nc.sync.wait_ge(sxn2.h, xn2_store_d0[0])
            xl = {0: dma(nc.sync, xn2b[0][:], xn2_scr[0:128, :], sxl[0])}
            evac_done = {}
            scan_done = {}
            u_done = {}
            st_done = {}
            mm_done = {}
            nc.tensor.wait_ge(wb.h, v_wb)
            for t in range(NT):
                if t + 1 < NT:
                    if t >= 1:
                        nc.sync.wait_ge(pe.h, mm_done[(t - 1, 23)])
                    nc.sync.wait_ge(sxn2.h, xn2_store_d0[t + 1])
                    xl[t + 1] = dma(nc.sync, xn2b[(t + 1) % 2][:],
                                    xn2_scr[128 * (t + 1):128 * (t + 2), :], sxl[(t + 1) % 2])
                nc.tensor.wait_ge(sxl[t % 2].h, xl[t])
                mm_order = list(range(8, 16)) + list(range(0, 8)) + list(range(16, 24))
                for mi, m in enumerate(mm_order):
                    j = mi % 4
                    if mi >= 4:
                        nc.tensor.wait_ge(act.h, evac_done[(t, mm_order[mi - 4])])
                    elif t > 0:
                        nc.tensor.wait_ge(act.h, evac_done[(t - 1, mm_order[20 + mi])])
                    for k in range(DC):
                        mma = nc.tensor.matmul(pa[j][:], Wbt[:, k * H3 + 128 * m:k * H3 + 128 * (m + 1)],
                                               xn2b[t % 2][:, k * T:(k + 1) * T],
                                               start=(k == 0), stop=(k == DC - 1))
                    mm_done[(t, m)] = tick(mma, pe)
                    c = m % 8
                    csl = slice(c * T, (c + 1) * T)
                    scl.wait_ge(pe.h, mm_done[(t, m)])
                    if m >= 8 and m < 16:
                        if t > 0:
                            scl.wait_ge(dve.h, scan_done[(t - 1, c)])
                        scl.activation(out=ig_t[:, csl], in_=pa[j][:], func=AF.Sigmoid,
                                             bias=bias_t[:, m:m + 1])
                        evac_done[(t, m)] = tick(scl.activation(
                            out=f_t[:, csl], in_=pa[j][:], func=AF.Sigmoid,
                            scale=-1.0, bias=negb_t[:, m:m + 1]), act)
                    elif m < 8:
                        if t > 0:
                            scl.wait_ge(dve.h, u_done[(t - 1, c)])
                        evac_done[(t, m)] = tick(scl.activation(
                            out=th_t[:, csl], in_=pa[j][:], func=AF.Tanh,
                            bias=bias_t[:, m:m + 1]), act)
                        vec.wait_ge(act.h, evac_done[(t, m)])
                        u_done[(t, c)] = tick(vec.tensor_tensor(
                            out=u_t[:, csl], in0=th_t[:, csl], in1=ig_t[:, csl],
                            op=ALU.mult), dve)
                    else:
                        if t > 0:
                            scl.wait_ge(sast.h, st_done[t - 1])
                        evac_done[(t, m)] = tick(scl.activation(
                            out=og_st[:, csl], in_=pa[j][:], func=AF.Sigmoid,
                            bias=bias_t[:, m:m + 1]), act)
                for c in range(DC):
                    csl = slice(c * T, (c + 1) * T)
                    vec.wait_ge(act.h, evac_done[(t, 8 + c)])
                    if t > 0:
                        vec.wait_ge(sast.h, st_done[t - 1])
                    vec.tensor_tensor_scan(out=hl_st[:, csl], data0=f_t[:, csl],
                                                 data1=u_t[:, csl],
                                                 initial=state_h[:, c:c + 1],
                                                 op0=ALU.mult, op1=ALU.add)
                    vec.tensor_copy(out=state_h[:, c:c + 1], in_=hl_st[:, c * T + T - 1:c * T + T])
                    vec.tensor_tensor_scan(out=pp_st[:, csl], data0=f_t[:, csl],
                                                 data1=zerosT[:],
                                                 initial=state_p[:, c:c + 1],
                                                 op0=ALU.mult, op1=ALU.add)
                    scan_done[(t, c)] = tick(vec.tensor_copy(
                        out=state_p[:, c:c + 1], in_=pp_st[:, c * T + T - 1:c * T + T]), dve)
                nc.sync.wait_ge(dve.h, scan_done[(t, DC - 1)])
                nc.sync.wait_ge(act.h, evac_done[(t, 23)])
                dma(nc.sync, hl_scr[128 * t:128 * (t + 1), :], hl_st[:], sast)
                dma(nc.sync, pp_scr[128 * t:128 * (t + 1), :], pp_st[:], sast)
                st_done[t] = dma(nc.sync, og_scr[128 * t:128 * (t + 1), :], og_st[:], sast)

            pre_carry = fence_snapshot()
            v_c0 = tick(vec.tensor_scalar(out=h_in_t[:], in0=state_h[:],
                                                scalar1=flv[:, 0:1], scalar2=None,
                                                op0=ALU.mult), dve)
            nc.tensor.wait_ge(dve.h, v_c0)
            v_c1 = tick(nc.tensor.transpose(pcar[:], h_in_t[:], identf[:]), pe)
            vec.wait_ge(pe.h, v_c1)
            v_c2 = tick(vec.tensor_copy(out=carr_sb[:], in_=pcar[:]), dve)
            nc.gpsimd.wait_ge(dve.h, v_c2)
            v_c3 = dma(nc.gpsimd, carry_i, carr_sb[:], qg)
            nc.gpsimd.wait_ge(qg.h, v_c3)
            v_c4 = tick(nc.gpsimd.collective_compute(
                "AllReduce", ALU.add, replica_groups=[[0, 1], [2, 3], [4, 5], [6, 7]],
                ins=[carry_i], outs=[carry_o]), cc)
            nc.gpsimd.wait_ge(cc.h, v_c4)
            v_c5 = dma(nc.gpsimd, carr_sb[:], carry_o, qg)
            nc.tensor.wait_ge(qg.h, v_c5)
            v_c6 = tick(nc.tensor.transpose(pcar2[:], carr_sb[:], identf[0:8, 0:8]), pe)
            vec.wait_ge(pe.h, v_c6)
            tick(vec.tensor_scalar(out=h_in_t[:], in0=pcar2[:], scalar1=flv[:, 1:2],
                                         scalar2=None, op0=ALU.mult), dve)
        queue_fence(pre_carry)

        seg2 = ExitStack()
        XN["t"] = seg2.enter_context(nc.sbuf_tensor("xn_seg2", [128, DC * TOK], BF16))
        with ExitStack() as bctx:
            hlb = [bctx.enter_context(nc.sbuf_tensor(f"hlb{j}", [128, TOK], BF16)) for j in range(2)]
            ppb = [bctx.enter_context(nc.sbuf_tensor(f"ppb{j}", [128, TOK], BF16)) for j in range(2)]
            ogb = [bctx.enter_context(nc.sbuf_tensor(f"ogb{j}", [128, TOK], BF16)) for j in range(2)]
            x1b = [bctx.enter_context(nc.sbuf_tensor(f"x1b{j}", [128, TOK], BF16)) for j in range(2)]
            htr = bctx.enter_context(nc.sbuf_tensor("htr", [128, TOK], BF16))
            thb = bctx.enter_context(nc.sbuf_tensor("thb", [128, TOK], BF16))
            x2st = [bctx.enter_context(nc.sbuf_tensor(f"x2st{j}", [128, TOK], BF16)) for j in range(2)]
            sqb3 = [bctx.enter_context(nc.sbuf_tensor(f"sqb3_{j}", [128, TOK], BF16)) for j in range(2)]
            m_t3 = bctx.enter_context(nc.sbuf_tensor("m_t3", [128, T], F32))
            var3 = bctx.enter_context(nc.sbuf_tensor("var3", [128, T], F32))
            rstd3 = bctx.enter_context(nc.sbuf_tensor("rstd3", [128, T], F32))
            mb3 = bctx.enter_context(nc.sbuf_tensor("mb3", [128, T], BF16))
            rstdb3 = bctx.enter_context(nc.sbuf_tensor("rstdb3", [128, T], BF16))
            pstat3 = [bctx.enter_context(nc.psum_tensor(f"pstat3_{j}", [128, 2 * T], F32))
                      for j in range(2)]
            lds = {}
            x2_done = {}
            stats_pe = {}
            ln3_dve = {}
            for t in range(2):
                lds[t] = [dma(nc.sync, hlb[t][:], hl_scr[128 * t:128 * (t + 1), :], spb[t]),
                          dma(nc.sync, ppb[t][:], pp_scr[128 * t:128 * (t + 1), :], spb[t]),
                          dma(nc.sync, ogb[t][:], og_scr[128 * t:128 * (t + 1), :], spb[t]),
                          dma(nc.sync, x1b[t][:], x1_scr[128 * t:128 * (t + 1), :], spb[t])]
            def pb_front(t):
                j = t % 2
                vec.wait_ge(spb[j].h, lds[t][-1])
                if t > 1:
                    vec.wait_ge(pe.h, stats_pe[t - 2])
                    vec.wait_ge(sx2p[j].h, x2_done[t - 2])
                htr3 = htr[:].rearrange("p (c n) -> p c n", n=T)
                hinb = h_in_t[:].unsqueeze(2).broadcast_to([128, DC, T])
                vec.tensor_tensor(out=htr3, in0=ppb[j][:].rearrange("p (c n) -> p c n", n=T),
                                  in1=hinb, op=ALU.mult)
                v_h = tick(vec.tensor_tensor(out=htr[:], in0=htr[:], in1=hlb[j][:],
                                             op=ALU.add), dve)
                scl.wait_ge(dve.h, v_h)
                v_th = tick(scl.activation(out=thb[:], in_=htr[:], func=AF.Tanh), act)
                if t > 0:
                    pb_back(t - 1)
                vec.wait_ge(act.h, v_th)
                vec.tensor_tensor(out=thb[:], in0=thb[:], in1=ogb[j][:], op=ALU.mult)
                tick(vec.tensor_tensor(out=x2st[j][:], in0=thb[:], in1=x1b[j][:],
                                       op=ALU.add), dve)
                v_sq = tick(vec.tensor_tensor(out=sqb3[j][:], in0=x2st[j][:],
                                              in1=x2st[j][:], op=ALU.mult), dve)
                v_sq_t[t] = v_sq
                nc.sync.wait_ge(dve.h, v_sq)
                x2_done[t] = dma(nc.sync, x2_scr[128 * t:128 * (t + 1), :], x2st[j][:], sx2p[j])
                if t + 2 < NT:
                    tt = t + 2
                    lds[tt] = [dma(nc.sync, hlb[j][:], hl_scr[128 * tt:128 * (tt + 1), :], spb[j]),
                               dma(nc.sync, ppb[j][:], pp_scr[128 * tt:128 * (tt + 1), :], spb[j]),
                               dma(nc.sync, ogb[j][:], og_scr[128 * tt:128 * (tt + 1), :], spb[j]),
                               dma(nc.sync, x1b[j][:], x1_scr[128 * tt:128 * (tt + 1), :], spb[j])]
                nc.tensor.wait_ge(dve.h, v_sq)
                if t > 1:
                    nc.tensor.wait_ge(dve.h, ln3_dve[t - 2])
                for c in range(DC):
                    nc.tensor.matmul(pstat3[j][:, 0:T], ones128[:], x2st[j][:, c * T:(c + 1) * T],
                                     start=(c == 0), stop=(c == DC - 1))
                for c in range(DC):
                    mm3 = nc.tensor.matmul(pstat3[j][:, T:2 * T], ones128[:],
                                           sqb3[j][:, c * T:(c + 1) * T],
                                           start=(c == 0), stop=(c == DC - 1))
                stats_pe[t] = tick(mm3, pe)

            def pb_back(t):
                j = t % 2
                vec.wait_ge(pe.h, stats_pe[t])
                vec.tensor_scalar(out=m_t3[:], in0=pstat3[j][:, 0:T], scalar1=1.0 / D,
                                  scalar2=None, op0=ALU.mult)
                vec.tensor_scalar(out=var3[:], in0=pstat3[j][:, T:2 * T], scalar1=1.0 / D,
                                  scalar2=EPS, op0=ALU.mult, op1=ALU.add)
                vec.tensor_tensor(out=rstd3[:], in0=m_t3[:], in1=m_t3[:], op=ALU.mult)
                v_v = tick(vec.tensor_tensor(out=var3[:], in0=var3[:], in1=rstd3[:],
                                             op=ALU.subtract), dve)
                scl.wait_ge(dve.h, v_v)
                scl.activation(out=rstd3[:], in_=var3[:], func=AF.Ln)
                v_r = tick(scl.activation(out=rstd3[:], in_=rstd3[:], func=AF.Exp,
                                          scale=-0.5), act)
                vec.wait_ge(act.h, v_r)
                vec.tensor_copy(out=mb3[:], in_=m_t3[:])
                vec.tensor_copy(out=rstdb3[:], in_=rstd3[:])
                xo3 = xnsl(t).rearrange("p (c n) -> p c n", n=T)
                mbb = mb3[:].unsqueeze(1).broadcast_to([128, DC, T])
                rbb = rstdb3[:].unsqueeze(1).broadcast_to([128, DC, T])
                vec.tensor_tensor(out=xo3, in0=x2st[j][:].rearrange("p (c n) -> p c n", n=T),
                                  in1=mbb, op=ALU.subtract)
                ln3_dve[t] = tick(vec.tensor_tensor(out=xo3, in0=xo3, in1=rbb,
                                                    op=ALU.mult), dve)
                tick(vec.tensor_reduce(out=amax8[2][:, t:t + 1], in_=xnsl(t),
                                       axis=AX.X, op=ALU.max,
                                       apply_absolute_value=True), dve)

            v_sq_t = {}
            for t in range(NT):
                pb_front(t)
            pb_back(NT - 1)

        barrier()

        with ExitStack() as octx1:
            wgu_sb1, v_wr1 = load_wgu(1, octx1)
            global_max_barrier(2, amax8[2], NT)
            vec.tensor_scalar(out=sgc[1][:], in0=sg_t[:], scalar1=sv[2][:, 0:1],
                              scalar2=None, op0=ALU.mult)
            v_sc1 = tick(vec.tensor_scalar(out=suc[1][:], in0=su_t[:], scalar1=sv[2][:, 0:1],
                                           scalar2=None, op0=ALU.mult), dve)
            gu_phase(1, wgu_sb1, v_wr1, amax8[3], sv[2], v_sc1)
        seg2.close()
        pre2 = fence_snapshot()
        global_max_barrier(3, amax8[3], NT * IC)
        tick(vec.tensor_scalar(out=sdc[1][:], in0=sd_t[:], scalar1=sv[3][:, 0:1],
                                     scalar2=0.5, op0=ALU.mult, op1=ALU.mult), dve)
        queue_fence(pre2)
        down_phase(1)
        barrier()

    return nc


_NC_CACHE = {}


def kernel(L0_attn_x, L0_attn_W, L0_attn_b, L0_ffn_clip_x, L0_ffn_clip_down,
           L0_ffn_w_gate, L0_ffn_w_up, L0_ffn_w_down):
    x = np.asarray(L0_attn_x, dtype=np.float32)
    W = np.ascontiguousarray(np.asarray(L0_attn_W, dtype=np.float32))
    b = np.ascontiguousarray(np.asarray(L0_attn_b, dtype=np.float32).reshape(PC, 128))
    wg = np.ascontiguousarray(np.asarray(L0_ffn_w_gate, dtype=np.float32))
    wu = np.ascontiguousarray(np.asarray(L0_ffn_w_up, dtype=np.float32))
    wd = np.ascontiguousarray(np.asarray(L0_ffn_w_down, dtype=np.float32))
    cx = float(np.asarray(L0_ffn_clip_x).reshape(()))
    cd = float(np.asarray(L0_ffn_clip_down).reshape(()))
    clips = np.array([[cx, cd]], dtype=np.float32)
    ident = np.eye(128, dtype=np.float32)

    if "nc" not in _NC_CACHE:
        _NC_CACHE["nc"] = build()
    nc = _NC_CACHE["nc"]

    in_maps = []
    for c in range(8):
        bb, k = c // 2, c % 2
        xs_ = np.ascontiguousarray(x[bb, TOK * k:TOK * (k + 1), :].T)
        in_maps.append({
            "x_fm": xs_, "w_gate": wg, "w_up": wu, "w_down": wd,
            "W_attn": W, "b_attn": b, "clips": clips,
            "flags": np.array([[1.0 - k, float(k)]], dtype=np.float32),
            "identin": ident,
        })
    res = run_bass_kernel_spmd(nc, in_maps, core_ids=list(range(8)))
    out = np.empty((4, 2 * TOK, D), dtype=np.float32)
    for c in range(8):
        bb, k = c // 2, c % 2
        out[bb, TOK * k:TOK * (k + 1), :] = np.asarray(res.results[c]["out_fm"]).T
    return out


if __name__ == "__main__":
    nc = build()
    n = sum(1 for _ in nc.all_instructions())
    print("built ok, instructions:", n)
